# version 26
# baseline (speedup 1.0000x reference)
import numpy as np
from contextlib import ExitStack

import concourse.bass as bass
import concourse.mybir as mybir
from concourse.bass_utils import run_bass_kernel_spmd

F32 = mybir.dt.float32
BF = mybir.dt.bfloat16
I8 = mybir.dt.int8
AF = mybir.ActivationFunctionType
ALU = mybir.AluOpType

D, C, H = 1024, 2048, 3072
NDC, NCC, NOC, NHC = 8, 16, 8, 24
TT = 512
NQ = TT // 128
EPS = 1e-5
INVD = 1.0 / D
INV127 = 1.0 / 127.0


def build_nc(T):
    NT = T // TT
    nc = bass.Bass("TRN2", target_bir_lowering=False,
                   dynamic_dma_scratch_size=1024)

    hT = nc.dram_tensor("hT", [D, T], F32, kind="ExternalInput").ap()
    w_in_img = nc.dram_tensor("w_in_img", [NCC, 128, 1024], F32, kind="ExternalInput").ap()
    wo_img = nc.dram_tensor("wo_img", [NOC, 128, 2048], F32, kind="ExternalInput").ap()
    wg_img = nc.dram_tensor("wg_img", [NHC, 128, 1024], I8, kind="ExternalInput").ap()
    wu_img = nc.dram_tensor("wu_img", [NHC, 128, 1024], I8, kind="ExternalInput").ap()
    wd_img = nc.dram_tensor("wd_img", [NHC, 128, 1024], I8, kind="ExternalInput").ap()
    cw_sb = nc.dram_tensor("cw_sb", [128, 64], F32, kind="ExternalInput").ap()
    cb_sb = nc.dram_tensor("cb_sb", [128, 16], F32, kind="ExternalInput").ap()
    ident = nc.dram_tensor("ident", [128, 128], F32, kind="ExternalInput").ap()
    out = nc.dram_tensor("out", [T, D], F32, kind="ExternalOutput").ap()
    wi_hl = nc.dram_tensor("wi_hl", [NCC, 2, 128, 1024], BF, kind="Internal").ap()
    wo_hl = nc.dram_tensor("wo_hl", [NOC * 4, 128, 1024], BF, kind="Internal").ap()

    A = nc.alloc_sbuf_tensor
    s_h = A("s_h", [128, 4096], F32)
    s_lnh = A("s_lnh", [128, 4096], BF)
    s_lnl = A("s_lnl", [128, 4096], BF)
    s_x1 = A("s_x1", [128, 4096], F32)
    s_sh = A("s_sh", [128, 8192], BF)
    s_sl = A("s_sl", [128, 8192], BF)
    s_mh = A("s_mh", [128, 12288], BF)
    s_ml = A("s_ml", [128, 12288], BF)
    s_u = A("s_u", [128, 516], F32)
    s_carry = A("s_carry", [128, 48], F32)
    s_wi = [[A(f"s_wi{i}_{d}", [128, 512], F32) for d in range(2)]
            for i in range(2)]
    s_wiP = [[A(f"s_wiP{i}_{d}", [128, 1024], BF) for d in range(4)]
             for i in range(2)]
    s_wo = [A(f"s_wo{i}", [128, 512], F32) for i in range(2)]
    s_woP = [A(f"s_woP{i}", [128, 1024], BF) for i in range(6)]
    s_wg8 = [A(f"s_wg8{i}", [128, 1024], I8) for i in range(2)]
    s_wu8 = [A(f"s_wu8{i}", [128, 1024], I8) for i in range(2)]
    s_wd8 = [A(f"s_wd8{i}", [128, 1024], I8) for i in range(2)]
    s_wgf = [A(f"s_wgf{i}", [128, 1024], BF) for i in range(2)]
    s_wuf = [A(f"s_wuf{i}", [128, 1024], BF) for i in range(2)]
    s_wdf = [A(f"s_wdf{i}", [128, 1024], BF) for i in range(2)]
    s_out = [A(f"s_out{i}", [128, 512], F32) for i in range(2)]
    s_rbc = A("s_rbc", [128, 512], F32)
    s_nmr = A("s_nmr", [128, 512], F32)
    s_wsum = A("s_wsum", [128, 16], F32)
    s_sq = [A(f"s_sq{i}", [128, 512], F32) for i in range(2)]
    s_sv = [A(f"s_sv{i}", [128, 512], F32) for i in range(2)]
    s_uv = [A(f"s_uv{i}", [128, 512], F32) for i in range(2)]
    s_wsg = A("s_wsg", [128, 24], F32)
    s_wsu = A("s_wsu", [128, 24], F32)
    s_ones_bf = A("s_ones_bf", [128, 1], BF)
    s_id = A("s_id", [128, 128], F32)
    s_cw = A("s_cw", [128, 64], F32)
    s_cb = A("s_cb", [128, 16], F32)
    s_ones = A("s_ones", [128, 1], F32)
    s_ones1 = A("s_ones1", [1, 128], F32)
    s_nones1 = A("s_nones1", [1, 128], F32)
    s_rows = A("s_rows", [1, 2048], F32)
    s_eps = A("s_eps", [1, 1], F32)

    pb = [nc.alloc_psum_tensor(f"pb{i}", [128, 512], F32) for i in range(8)]
    f2pair = [(4, 5), (6, 7), (2, 3), (0, 1)]

    ops = {k: [] for k in ("tensor", "vector", "scalar", "sync", "gpsimd")}

    def em(eng, fn):
        ops[eng].append(fn)

    cnt = {"pe": 0, "ve": 0, "ae": 0, "cst": 0, "h": 0,
           "wiA0": 0, "wiA1": 0, "wiA2": 0, "wiA3": 0,
           "wiB0": 0, "wiB1": 0, "wiB2": 0, "wiB3": 0,
           "wo0": 0, "wo1": 0, "wo2": 0, "wo3": 0, "wo4": 0, "wo5": 0,
           "wg0": 0, "wg1": 0, "wu0": 0, "wu1": 0, "wd0": 0, "wd1": 0,
           "st0": 0, "st1": 0, "gp": 0, "wir0": 0, "wir1": 0,
           "wor0": 0, "wor1": 0}
    ev = {}

    def inc(c, n=1):
        cnt[c] += n
        return cnt[c]

    with ExitStack() as es:
        sem = {k: es.enter_context(nc.semaphore(k)) for k in cnt}

        def wt(eng, c, val):
            if val > 0:
                em(eng, lambda e, s=sem[c], v=val: e.wait_ge(s, v))

        def drain(eng="vector"):
            em(eng, lambda e: e.drain())

        for buf, val in ((s_carry, 0.0), (s_ones, 1.0), (s_ones1, 1.0),
                         (s_nones1, -1.0), (s_eps, EPS), (s_ones_bf, 1.0)):
            em("vector", lambda e, buf=buf, val=val, s=sem["ve"]: e.memset(
                buf[:], val).then_inc(s, 1))
            inc("ve")
        ev["init_ve"] = cnt["ve"]
        for dst, src in ((s_id, ident), (s_cw, cw_sb), (s_cb, cb_sb)):
            em("sync", lambda e, dst=dst, src=src, s=sem["cst"]: e.dma_start(
                dst[:], src).then_inc(s, 16))
            inc("cst", 16)
        ev["consts"] = cnt["cst"]

        for t in range(NT):
            t0 = t * TT

            def emit_hload(tp):
                if tp > 0:
                    wt("sync", "ve", ev[("x1", tp - 1, NOC - 1)])
                for j in range(NDC):
                    em("sync", lambda e, j=j, tp=tp, s=sem["h"]: e.dma_start(
                        s_h[:, 512 * j:512 * j + 512],
                        hT[128 * j:128 * j + 128, tp * TT:tp * TT + TT]
                        ).then_inc(s, 16))
                    inc("h", 16)
                ev[("hldall", tp)] = cnt["h"]

            sqb = [[s_sq[0], s_sq[1], s_sv[0], s_sv[1], s_uv[0], s_uv[1]],
                   [s_sq[0], s_sq[1]]]

            def emit_sq(t, which, j, src):
                bufs = sqb[which]
                nb = len(bufs)
                if which == 0:
                    if t > 0 and j < 2:
                        wt("scalar", "ve", ev[("msplit", t - 1, 22 + j)])
                    if t > 0 and j == 2:
                        wt("scalar", "ve", ev[("msplit", t - 1, NHC - 1)])
                    wt("scalar", "h", ev[("hldall", t)])
                else:
                    wt("scalar", "ve", ev[("x1", t, j)])
                if j >= nb:
                    wt("scalar", "pe", ev[("mmsq", t, which, j - nb)])
                em("scalar", lambda e, j=j, src=src, buf=bufs[j % nb],
                   s=sem["ae"]: e.activation(
                    buf[:], src[:, 512 * j:512 * j + 512],
                    AF.Square).then_inc(s, 1))
                ev[("sq", t, which, j)] = inc("ae")

            def emit_pe_statspair(t, which, j, src):
                if which == 0:
                    if t == 0 and j == 0:
                        wt("tensor", "ve", ev["init_ve"])
                    wt("tensor", "h", ev[("hldall", t)])
                    if t > 0 and j == 0:
                        wt("tensor", "ve", ev[("f2evac", t - 1, 3)])
                else:
                    wt("tensor", "ve", ev[("x1", t, j)])
                    if j == 0:
                        wt("tensor", "ve", ev[("var", t, 0)])
                em("tensor", lambda e, j=j, src=src: e.matmul(
                    pb[0][0:1, :], s_ones[:], src[:, 512 * j:512 * j + 512],
                    start=(j == 0), stop=(j == NDC - 1)))
                wt("tensor", "ae", ev[("sq", t, which, j)])
                em("tensor", lambda e, j=j, s=sem["pe"],
                   buf=sqb[which][j % len(sqb[which])]: e.matmul(
                    pb[1][0:1, :], s_ones[:], buf[:],
                    start=(j == 0), stop=(j == NDC - 1)).then_inc(s, 1))
                ev[("mmsq", t, which, j)] = inc("pe")

            def emit_hsplit1(t, j):
                wt("scalar", "h", ev[("hldall", t)])
                wt("scalar", "pe", ev[("up", t - 1, NHC - 1)] if t > 0 else 0)
                em("scalar", lambda e, j=j, s=sem["ae"]: e.copy(
                    s_lnh[:, 512 * j:512 * j + 512],
                    s_h[:, 512 * j:512 * j + 512]).then_inc(s, 1))
                ev[("hhcopy", t, j)] = inc("ae")
                wt("vector", "ae", ev[("hhcopy", t, j)])
                em("vector", lambda e, j=j, s=sem["ve"]: e.tensor_sub(
                    s_lnl[:, 512 * j:512 * j + 512],
                    s_h[:, 512 * j:512 * j + 512],
                    s_lnh[:, 512 * j:512 * j + 512]).then_inc(s, 1))
                ev[("hsplit", t, j)] = inc("ve")

            def emit_ln(t, which, src, prefix_done=False):
                for j in range(NDC):
                    if not (prefix_done and j < 6):
                        emit_sq(t, which, j, src)
                    if which == 0 and not prefix_done:
                        emit_hsplit1(t, j)
                    emit_pe_statspair(t, which, j, src)
                emit_ln_tail(t, which, src)

            def emit_bcast(t, which):
                row_r = s_rows[0:1, 0:512]
                row_mr = s_rows[0:1, 512:1024]
                wt("tensor", "ve", ev[("rows", t, which)])
                if which == 0 and t > 0:
                    wt("tensor", "ve", ev[("f2evac", t - 1, 2)])
                em("tensor", lambda e, row_r=row_r: e.matmul(
                    pb[2][:], s_ones1[:], row_r, start=True, stop=True))
                em("tensor", lambda e, row_mr=row_mr, s=sem["pe"]: e.matmul(
                    pb[3][:], s_nones1[:], row_mr, start=True, stop=True).then_inc(s, 1))
                ev[("bcast", t, which)] = inc("pe")
                wt("vector", "pe", ev[("bcast", t, which)])
                em("vector", lambda e: e.tensor_copy(s_rbc[:], pb[2][:]))
                em("vector", lambda e, s=sem["ve"]: e.tensor_copy(
                    s_nmr[:], pb[3][:]).then_inc(s, 1))
                ev[("rbccopy", t) if which == 0 else ("rbccopy2", t)] = inc("ve")

            def emit_ln_tail(t, which, src):

                row_r = s_rows[0:1, 0:512]
                row_mr = s_rows[0:1, 512:1024]
                row_aux = s_rows[0:1, 1024:1536]
                row_var = s_rows[0:1, 1536:2048]
                wt("vector", "pe", ev[("mmsq", t, which, NDC - 1)])
                em("vector", lambda e, row_aux=row_aux: e.tensor_scalar(
                    row_aux, pb[0][0:1, :], INVD, None, ALU.mult))
                drain()
                em("vector", lambda e, row_aux=row_aux, row_r=row_r: e.tensor_mul(
                    row_r, row_aux, row_aux))
                drain()
                em("vector", lambda e, row_r=row_r, row_var=row_var, s=sem["ve"]:
                   e.scalar_tensor_tensor(
                    row_var, pb[1][0:1, :], INVD, row_r, ALU.mult, ALU.subtract
                    ).then_inc(s, 1))
                ev[("var", t, which)] = inc("ve")
                wt("scalar", "ve", ev[("var", t, which)])
                em("scalar", lambda e, row_r=row_r, row_var=row_var, s=sem["ae"]:
                   e.activation(
                    row_r, row_var, AF.Sqrt, bias=s_eps[0:1, :]).then_inc(s, 1))
                ev[("sd", t, which)] = inc("ae")
                wt("vector", "ae", ev[("sd", t, which)])
                em("vector", lambda e, row_r=row_r, row_var=row_var: e.reciprocal(
                    row_var, row_r))
                drain()
                em("vector", lambda e, row_r=row_r, row_var=row_var,
                   sc=(1.0 if which == 0 else INV127): e.tensor_scalar(
                    row_r, row_var, sc, None, ALU.mult))
                drain()
                em("vector", lambda e, row_r=row_r, row_mr=row_mr, row_aux=row_aux,
                   s=sem["ve"]: e.tensor_mul(row_mr, row_aux, row_r).then_inc(s, 1))
                ev[("rows", t, which)] = inc("ve")

            def emit_ssplit_hi(t, cc):
                wt("scalar", "ae", ev[("silu", t, cc)])
                em("scalar", lambda e, cc=cc, s=sem["ae"]: e.copy(
                    s_sh[:, 512 * cc:512 * cc + 512], s_sv[cc % 2][:]).then_inc(s, 1))
                ev[("shcopy", t, cc)] = inc("ae")

            def emit_ssplit_lo(t, cc):
                wt("gpsimd", "ae", ev[("shcopy", t, cc)])
                em("gpsimd", lambda e, cc=cc, s=sem["gp"]: e.tensor_sub(
                    s_sl[:, 512 * cc:512 * cc + 512], s_sv[cc % 2][:],
                    s_sh[:, 512 * cc:512 * cc + 512]).then_inc(s, 1))
                ev[("ssplit", t, cc)] = inc("gp")

            def emit_silu(t, cc):
                wt("scalar", "ve", ev[("conv", t, cc)])
                if t == 0 and cc == 0:
                    wt("scalar", "cst", ev["consts"])
                if cc >= 2:
                    wt("scalar", "gp", ev[("ssplit", t, cc - 2)])
                em("scalar", lambda e, cc=cc, s=sem["ae"]: e.activation(
                    s_sv[cc % 2][:], s_sq[cc % 2][:],
                    AF.Silu, bias=s_cb[:, cc:cc + 1]).then_inc(s, 1))
                ev[("silu", t, cc)] = inc("ae")
                emit_ssplit_hi(t, cc)

            if t == 0:
                emit_hload(0)
                emit_ln(0, 0, s_h)

            def emit_convblock(t, cc):
                bank = pb[4 + (cc % 2 if t == 0 else cc % 4)]
                wt("vector", "pe", ev[("ip", t, cc)])
                if t == 0 and cc == 0:
                    wt("vector", "cst", ev["consts"])
                wt("vector", "ve", ev[("rbccopy", t)])
                if t == 0:
                    wt("vector", "ve", ev[("wsumcol", t, cc)])
                if cc > 0:
                    wt("vector", "ve", ev[("conv", t, cc - 1)])
                elif t > 0:
                    wt("vector", "ve", ev[("conv", t - 1, NCC - 1)])
                em("vector", lambda e, cc=cc: e.tensor_copy(
                    s_u[:, 0:3], s_carry[:, 3 * cc:3 * cc + 3]))
                em("vector", lambda e, bank=bank: e.tensor_mul(
                    s_u[:, 3:515], bank[:], s_rbc[:]))
                drain()
                em("vector", lambda e, cc=cc, s=sem["ve"]: e.scalar_tensor_tensor(
                    s_u[:, 3:515], s_nmr[:], s_wsum[:, cc:cc + 1], s_u[:, 3:515],
                    ALU.mult, ALU.add).then_inc(s, 1))
                ev[("uevac", t, cc)] = inc("ve")
                drain()
                em("vector", lambda e, cc=cc: e.tensor_copy(
                    s_carry[:, 3 * cc:3 * cc + 3], s_u[:, 512:515]))
                if cc >= 2:
                    wt("vector", "ae", ev[("silu", t, cc - 2)])
                em("vector", lambda e, cc=cc: e.tensor_scalar(
                    s_sq[cc % 2][:], s_u[:, 3:515], s_cw[:, 4 * cc + 3:4 * cc + 4],
                    None, ALU.mult))
                for k in (2, 1, 0):
                    drain()

                    def mk_conv(cc=cc, k=k, s=sem["ve"]):
                        def f(e):
                            i = e.scalar_tensor_tensor(
                                s_sq[cc % 2][:], s_u[:, k:k + 512],
                                s_cw[:, 4 * cc + k:4 * cc + k + 1], s_sq[cc % 2][:],
                                ALU.mult, ALU.add)
                            if k == 0:
                                i.then_inc(s, 1)
                        return f
                    em("vector", mk_conv())
                ev[("conv", t, cc)] = inc("ve")

            for cc in range(NCC):
                if t == 0:
                    for half, semname in ((0, "wiA%d" % (cc % 2)),
                                          (1, "wiB%d" % (cc % 2))):
                        if cc >= 2:
                            wt("sync", "gp", ev[("wsplit", 0, cc - 2, half)])
                            if ("wsumt", 0, cc - 2) in ev:
                                wt("sync", "pe", ev[("wsumt", 0, cc - 2)])
                        em("sync", lambda e, cc=cc, half=half, s=sem[semname]:
                           e.dma_start(
                            s_wi[half][cc % 2][:],
                            w_in_img[cc][:, 512 * half:512 * half + 512]
                            ).then_inc(s, 16))
                        ev[("widma", 0, cc, half)] = inc(semname, 16)

                    for half, semname in ((0, "wiA%d" % (cc % 2)),
                                          (1, "wiB%d" % (cc % 2))):
                        wt("scalar", semname, ev[("widma", 0, cc, half)])
                        if cc >= 2:
                            wt("scalar", "pe",
                               ev[("ipA" if half == 0 else "ipB", 0, cc - 2)])
                            wt("scalar", "gp", ev[("wsplit", 0, cc - 2, half)])
                            wt("scalar", "wir%d" % (cc % 2),
                               ev[("wist", cc - 2, 1)])
                        em("scalar", lambda e, cc=cc, half=half, s=sem["ae"]:
                           e.copy(s_wiP[half][cc % 2][:, 0:512],
                                  s_wi[half][cc % 2][:]).then_inc(s, 1))
                        ev[("whcopy", 0, cc, half)] = inc("ae")
                        wt("gpsimd", "ae", ev[("whcopy", 0, cc, half)])
                        em("gpsimd", lambda e, cc=cc, half=half, s=sem["gp"]:
                           e.tensor_sub(
                            s_wiP[half][cc % 2][:, 512:1024],
                            s_wi[half][cc % 2][:],
                            s_wiP[half][cc % 2][:, 0:512]).then_inc(s, 1))
                        ev[("wsplit", 0, cc, half)] = inc("gp")
                        wt("scalar", "gp", ev[("wsplit", 0, cc, half)])
                        em("scalar", lambda e, cc=cc, half=half,
                           s=sem["wir%d" % (cc % 2)]: e.dma_start(
                            wi_hl[cc, half], s_wiP[half][cc % 2][:]
                            ).then_inc(s, 16))
                        ev[("wist", cc, half)] = inc("wir%d" % (cc % 2), 16)
                else:
                    for half, semname in ((0, "wiA%d" % (cc % 4)),
                                          (1, "wiB%d" % (cc % 4))):
                        ipn = "ipA" if half == 0 else "ipB"
                        eng = "sync" if half == 0 else "scalar"
                        if cc >= 4:
                            wt(eng, "pe", ev[(ipn, t, cc - 4)])
                        elif t == 1:
                            wt(eng, "pe", ev[(ipn, 0, NCC - 2 + (cc % 2))])
                            if cc < 2:
                                wt(eng, "wir%d" % (cc % 2),
                                   ev[("wist", NCC - 2 + cc, 1)])
                        else:
                            wt(eng, "pe", ev[(ipn, t - 1, NCC - 4 + cc)])
                        em(eng, lambda e, cc=cc, half=half, s=sem[semname]:
                           e.dma_start(s_wiP[half][cc % 4][:],
                                       wi_hl[cc, half]).then_inc(s, 16))
                        ev[("widma", t, cc, half)] = inc(semname, 16)

                if cc == 1:
                    emit_bcast(t, 0)

                if cc > 0:
                    emit_convblock(t, cc - 1)
                    emit_silu(t, cc - 1)
                    emit_ssplit_lo(t, cc - 1)

                nbk = 2 if t == 0 else 4
                bank = pb[4 + cc % nbk]
                if cc >= nbk:
                    wt("tensor", "ve", ev[("uevac", t, cc - nbk)])
                wdep = cc % 2 if t == 0 else cc % 4
                for j in range(NDC):
                    half, jj = (0, j) if j < 4 else (1, j - 4)
                    if j == 0 or j == 4:
                        if t == 0:
                            wt("tensor", "gp", ev[("wsplit", 0, cc, half)])
                        else:
                            wt("tensor", "wi%s%d" % ("AB"[half], cc % 4),
                               ev[("widma", t, cc, half)])
                    if cc < 2:
                        wt("tensor", "ve", ev[("hsplit", t, j)])
                    for p in range(3):
                        woff = 128 * jj if p < 2 else 512 + 128 * jj
                        xsrc = s_lnl if p == 1 else s_lnh
                        first = (j == 0 and p == 0)
                        last = (j == NDC - 1 and p == 2)
                        mark = (j == 3 and p == 2) or last

                        def mk_ip(j=j, woff=woff, wsrc=s_wiP[half][wdep],
                                  xsrc=xsrc, bank=bank,
                                  first=first, last=last, mark=mark, s=sem["pe"]):
                            def f(e):
                                i = e.matmul(bank[:],
                                             wsrc[:, woff:woff + 128],
                                             xsrc[:, 512 * j:512 * j + 512],
                                             start=first, stop=last)
                                if mark:
                                    i.then_inc(s, 1)
                            return f
                        em("tensor", mk_ip())
                        if j == 3 and p == 2:
                            ev[("ipA", t, cc)] = inc("pe")
                        if last:
                            ev[("ipB", t, cc)] = inc("pe")
                ev[("ip", t, cc)] = cnt["pe"]

                if t == 0:
                    if cc >= 2:
                        wt("tensor", "ve", ev[("wrow", t, cc - 2)])
                    for j8 in range(NDC):
                        half, jj = (0, j8) if j8 < 4 else (1, j8 - 4)

                        def mk_ws(cc=cc, half=half, jj=jj, j8=j8, s=sem["pe"]):
                            def f(e):
                                i = e.matmul(pb[6][0:1, 0:128], s_ones[:],
                                             s_wi[half][cc % 2][:, 128 * jj:128 * jj + 128],
                                             start=(j8 == 0), stop=(j8 == NDC - 1))
                                if j8 == NDC - 1:
                                    i.then_inc(s, 1)
                            return f
                        em("tensor", mk_ws())
                    ev[("wsum", t, cc)] = inc("pe")
                    wt("vector", "pe", ev[("wsum", t, cc)])
                    if cc == 0:
                        wt("vector", "ve", ev[("rows", t, 0)])
                    em("vector", lambda e, s=sem["ve"]: e.tensor_copy(
                        s_rows[0:1, 1024:1152], pb[6][0:1, 0:128]).then_inc(s, 1))
                    ev[("wrow", t, cc)] = inc("ve")
                    wt("tensor", "ve", ev[("wrow", t, cc)])
                    if cc >= 2:
                        wt("tensor", "ve", ev[("wsumcol", t, cc - 2)])
                    em("tensor", lambda e, s=sem["pe"]: e.matmul(
                        pb[7][:, 0:1], s_rows[0:1, 1024:1152], s_ones[0:1, 0:1],
                        start=True, stop=True).then_inc(s, 1))
                    ev[("wsumt", t, cc)] = inc("pe")
                    wt("vector", "pe", ev[("wsumt", t, cc)])
                    em("vector", lambda e, cc=cc, s=sem["ve"]: e.tensor_copy(
                        s_wsum[:, cc:cc + 1], pb[7][:, 0:1]).then_inc(s, 1))
                    ev[("wsumcol", t, cc)] = inc("ve")

            emit_convblock(t, NCC - 1)
            emit_silu(t, NCC - 1)
            emit_ssplit_lo(t, NCC - 1)

            obank = [2, 3, 6, 7]

            def emit_x1add(t, oc):
                wt("vector", "pe", ev[("opB", t, oc)])
                em("vector", lambda e, oc=oc, bank=pb[obank[oc % 4]],
                   s=sem["ve"]: e.tensor_add(
                    s_x1[:, 512 * oc:512 * oc + 512], bank[:],
                    s_h[:, 512 * oc:512 * oc + 512]).then_inc(s, 1))
                ev[("x1", t, oc)] = inc("ve")

            def emit_x1split(t, oc):
                wt("scalar", "ve", ev[("x1", t, oc)])
                em("scalar", lambda e, oc=oc, s=sem["ae"]: e.copy(
                    s_lnh[:, 512 * oc:512 * oc + 512],
                    s_x1[:, 512 * oc:512 * oc + 512]).then_inc(s, 1))
                ev[("x1hi", t, oc)] = inc("ae")
                wt("gpsimd", "ae", ev[("x1hi", t, oc)])
                em("gpsimd", lambda e, oc=oc, s=sem["gp"]: e.tensor_sub(
                    s_lnl[:, 512 * oc:512 * oc + 512],
                    s_x1[:, 512 * oc:512 * oc + 512],
                    s_lnh[:, 512 * oc:512 * oc + 512]).then_inc(s, 1))
                ev[("x1lo", t, oc)] = inc("gp")

            pending_add = None
            for qi in range(4 * NOC):
                oc, qt = divmod(qi, 4)
                if t == 0:
                    if qi >= 2:
                        wt("sync", "ve", ev[("wosplit", 0, qi - 2)])
                    em("sync", lambda e, oc=oc, qt=qt, qi=qi,
                       s=sem["wo%d" % (qi % 2)]:
                       e.dma_start(s_wo[qi % 2][:],
                                   wo_img[oc][:, 512 * qt:512 * qt + 512]
                                   ).then_inc(s, 16))
                    ev[("wodma", 0, qi)] = inc("wo%d" % (qi % 2), 16)

                    wt("scalar", "wo%d" % (qi % 2), ev[("wodma", 0, qi)])
                    if qi >= 2:
                        wt("scalar", "pe", ev[("opQ", 0, qi - 2)])
                        wt("scalar", "ve", ev[("wosplit", 0, qi - 2)])
                        wt("scalar", "wor%d" % (qi % 2), ev[("wost", qi - 2)])
                    em("scalar", lambda e, qi=qi, s=sem["ae"]: e.copy(
                        s_woP[qi % 2][:, 0:512], s_wo[qi % 2][:]).then_inc(s, 1))
                    ev[("wohcopy", 0, qi)] = inc("ae")
                    wt("vector", "ae", ev[("wohcopy", 0, qi)])
                    em("vector", lambda e, qi=qi, s=sem["ve"]: e.tensor_sub(
                        s_woP[qi % 2][:, 512:1024], s_wo[qi % 2][:],
                        s_woP[qi % 2][:, 0:512]).then_inc(s, 1))
                    ev[("wosplit", 0, qi)] = inc("ve")
                    wt("scalar", "ve", ev[("wosplit", 0, qi)])
                    em("scalar", lambda e, qi=qi, s=sem["wor%d" % (qi % 2)]:
                       e.dma_start(wo_hl[qi], s_woP[qi % 2][:]).then_inc(s, 16))
                    ev[("wost", qi)] = inc("wor%d" % (qi % 2), 16)
                else:
                    eng = "sync" if qi % 2 == 0 else "scalar"
                    if qi >= 6:
                        wt(eng, "pe", ev[("opQ", t, qi - 6)])
                    elif t == 1:
                        wt(eng, "pe", ev[("opQ", 0, 4 * NOC - 2 + (qi % 2))])
                        if qi < 2:
                            wt(eng, "wor%d" % (qi % 2),
                               ev[("wost", 4 * NOC - 2 + qi)])
                    else:
                        lastq = qi + 6 * ((4 * NOC - 1 - qi) // 6)
                        wt(eng, "pe", ev[("opQ", t - 1, lastq)])
                    em(eng, lambda e, qi=qi, s=sem["wo%d" % (qi % 6)]:
                       e.dma_start(s_woP[qi % 6][:], wo_hl[qi]).then_inc(s, 16))
                    ev[("wodma", t, qi)] = inc("wo%d" % (qi % 6), 16)

                if pending_add is not None:
                    emit_x1add(t, pending_add)
                    emit_sq(t, 1, pending_add, s_x1)
                    emit_x1split(t, pending_add)
                    pending_add = None

                bank = pb[obank[oc % 4]]
                if qt == 0:
                    if oc >= 4:
                        wt("tensor", "ve", ev[("x1", t, oc - 4)])
                    elif oc < 2:
                        wt("tensor", "ve", ev[("rbccopy", t)])
                    elif t > 0:
                        wt("tensor", "ve", ev[("uevac", t, NCC - 4 + oc)])
                if t == 0:
                    wt("tensor", "ve", ev[("wosplit", 0, qi)])
                else:
                    wt("tensor", "wo%d" % (qi % 6), ev[("wodma", t, qi)])
                wqdep = qi % 2 if t == 0 else qi % 6
                for kk in range(4):
                    k = 4 * qt + kk
                    if oc == 0:
                        wt("tensor", "gp", ev[("ssplit", t, k)])
                    for p in range(3):
                        woff = 128 * kk + (512 if p == 2 else 0)
                        xsrc = s_sl if p == 1 else s_sh
                        first = (k == 0 and p == 0)
                        last = (k == NCC - 1 and p == 2)
                        mark = (kk == 3 and p == 2)

                        def mk_op(k=k, woff=woff, wsrc=s_woP[wqdep],
                                  xsrc=xsrc, bank=bank,
                                  first=first, last=last, mark=mark, s=sem["pe"]):
                            def f(e):
                                i = e.matmul(bank[:],
                                             wsrc[:, woff:woff + 128],
                                             xsrc[:, 512 * k:512 * k + 512],
                                             start=first, stop=last)
                                if mark:
                                    i.then_inc(s, 1)
                            return f
                        em("tensor", mk_op())
                        if mark:
                            ev[("opQ", t, qi)] = inc("pe")

                if qt == 3:
                    ev[("opB", t, oc)] = cnt["pe"]
                    pending_add = oc
                elif qt == 2 and oc >= 1:
                    emit_pe_statspair(t, 1, oc - 1, s_x1)

            emit_x1add(t, NOC - 1)
            emit_sq(t, 1, NOC - 1, s_x1)
            emit_x1split(t, NOC - 1)
            emit_pe_statspair(t, 1, NOC - 1, s_x1)

            def emit_xt(q):
                a, b = f2pair[q]
                for j in range(NDC):
                    bank, col = (a, j) if j < 4 else (b, j - 4)
                    em("tensor", lambda e, q=q, j=j, bank=bank, col=col: e.matmul(
                        pb[bank][:, 128 * col:128 * col + 128],
                        s_x1[:, 512 * j + 128 * q:512 * j + 128 * q + 128],
                        s_id[:], start=(j % 4 == 0), stop=False))

            if t == 0:
                wt("tensor", "cst", ev["consts"])
            wt("tensor", "ve", ev[("x1", t, NOC - 1)])
            if t > 0:
                for q in range(2):
                    wt("tensor", "ve", ev[("f2evac", t - 1, q)])
            emit_xt(0)
            emit_xt(1)

            def emit_mcopy(t, hc):
                wt("scalar", "ve", ev[("mstt", t, hc)])
                em("scalar", lambda e, hc=hc, s=sem["ae"]: e.copy(
                    s_mh[:, 512 * hc:512 * hc + 512], s_sq[hc % 2][:]).then_inc(s, 1))
                ev[("mhcopy", t, hc)] = inc("ae")

            def emit_msplit_lo(t, hc):
                wt("vector", "ae", ev[("mhcopy", t, hc)])
                em("vector", lambda e, hc=hc, s=sem["ve"]: e.tensor_sub(
                    s_ml[:, 512 * hc:512 * hc + 512], s_sq[hc % 2][:],
                    s_mh[:, 512 * hc:512 * hc + 512]).then_inc(s, 1))
                ev[("msplit", t, hc)] = inc("ve")

            def emit_f1dma_cast(t, hc):
                if hc >= 2:
                    wt("sync", "ve", ev[("cwg", t, hc - 2)])
                elif t > 0:
                    wt("sync", "ve", ev[("cwg", t - 1, NHC - 2 + hc)])
                em("sync", lambda e, hc=hc, s=sem["wg%d" % (hc % 2)]: e.dma_start(
                    s_wg8[hc % 2][:], wg_img[hc]).then_inc(s, 16))
                ev[("wgdma", t, hc)] = inc("wg%d" % (hc % 2), 16)
                if hc >= 2:
                    wt("sync", "ae", ev[("cwu", t, hc - 2)])
                elif t > 0:
                    wt("sync", "ae", ev[("cwu", t - 1, NHC - 2 + hc)])
                em("sync", lambda e, hc=hc, s=sem["wu%d" % (hc % 2)]: e.dma_start(
                    s_wu8[hc % 2][:], wu_img[hc]).then_inc(s, 16))
                ev[("wudma", t, hc)] = inc("wu%d" % (hc % 2), 16)

                def lastw(kind, alt, tp, hcp):
                    return ev[(kind, tp, hcp)] if tp == 0 else ev[(alt, tp, hcp)]

                wt("vector", "wg%d" % (hc % 2), ev[("wgdma", t, hc)])
                if hc >= 2:
                    wt("vector", "pe", lastw("wsumg", "gate", t, hc - 2))
                elif t > 0:
                    wt("vector", "pe", lastw("wsumg", "gate", t - 1, NHC - 2 + hc))
                em("vector", lambda e, hc=hc, s=sem["ve"]: e.tensor_copy(
                    s_wgf[hc % 2][:], s_wg8[hc % 2][:]).then_inc(s, 1))
                ev[("cwg", t, hc)] = inc("ve")

                wt("scalar", "wu%d" % (hc % 2), ev[("wudma", t, hc)])
                if hc >= 2:
                    wt("scalar", "pe", lastw("wsumu", "up", t, hc - 2))
                elif t > 0:
                    wt("scalar", "pe", lastw("wsumu", "up", t - 1, NHC - 2 + hc))
                em("scalar", lambda e, hc=hc, s=sem["ae"]: e.copy(
                    s_wuf[hc % 2][:], s_wu8[hc % 2][:]).then_inc(s, 1))
                ev[("cwu", t, hc)] = inc("ae")

            emit_f1dma_cast(t, 0)
            emit_ln_tail(t, 1, s_x1)
            for hc in range(NHC):
                if hc + 1 < NHC:
                    emit_f1dma_cast(t, hc + 1)
                wt("tensor", "ve", ev[("cwg", t, hc)])
                if hc > 0:
                    wt("tensor", "ve", ev[("gtmp", t, hc - 1)])
                else:
                    wt("tensor", "ve", ev[("var", t, 1)])
                for j in range(NDC):
                    if hc == 0:
                        wt("tensor", "gp", ev[("x1lo", t, j)])
                    for p in range(2):
                        xsrc = s_lnh if p == 0 else s_lnl
                        first = (j == 0 and p == 0)
                        last = (j == NDC - 1 and p == 1)

                        def mk_gate(hc=hc, j=j, xsrc=xsrc, first=first, last=last,
                                    s=sem["pe"]):
                            def f(e):
                                i = e.matmul(pb[0][:],
                                             s_wgf[hc % 2][:, 128 * j:128 * j + 128],
                                             xsrc[:, 512 * j:512 * j + 512],
                                             start=first, stop=last)
                                if last:
                                    i.then_inc(s, 1)
                            return f
                        em("tensor", mk_gate())
                ev[("gate", t, hc)] = inc("pe")
                if hc == 0:
                    emit_bcast(t, 1)
                wt("tensor", "ae", ev[("cwu", t, hc)])
                if hc > 0:
                    wt("tensor", "ve", ev[("utmp", t, hc - 1)])
                for j in range(NDC):
                    for p in range(2):
                        xsrc = s_lnh if p == 0 else s_lnl
                        first = (j == 0 and p == 0)
                        last = (j == NDC - 1 and p == 1)

                        def mk_up(hc=hc, j=j, xsrc=xsrc, first=first, last=last,
                                  s=sem["pe"]):
                            def f(e):
                                i = e.matmul(pb[1][:],
                                             s_wuf[hc % 2][:, 128 * j:128 * j + 128],
                                             xsrc[:, 512 * j:512 * j + 512],
                                             start=first, stop=last)
                                if last:
                                    i.then_inc(s, 1)
                            return f
                        em("tensor", mk_up())
                ev[("up", t, hc)] = inc("pe")

                if t == 0:
                    if hc == 0:
                        wt("tensor", "ve", ev[("rbccopy2", 0)])
                    for wsname, wpar, bank, sdst in (
                            ("wsumg", s_wgf, pb[2], s_wsg),
                            ("wsumu", s_wuf, pb[3], s_wsu)):
                        for j8 in range(NDC):
                            def mk_ws(hc=hc, j8=j8, wpar=wpar, bank=bank,
                                      s=sem["pe"]):
                                def f(e):
                                    i = e.matmul(
                                        bank[:, hc:hc + 1],
                                        wpar[hc % 2][:, 128 * j8:128 * j8 + 128],
                                        s_ones_bf[:],
                                        start=(j8 == 0), stop=(j8 == NDC - 1))
                                    if j8 == NDC - 1:
                                        i.then_inc(s, 1)
                                return f
                            em("tensor", mk_ws())
                        ev[(wsname, 0, hc)] = inc("pe")
                        wt("vector", "pe", ev[(wsname, 0, hc)])
                        em("vector", lambda e, hc=hc, bank=bank, sdst=sdst,
                           s=sem["ve"]: e.tensor_copy(
                            sdst[:, hc:hc + 1], bank[:, hc:hc + 1]).then_inc(s, 1))
                        ev[(wsname + "c", 0, hc)] = inc("ve")

                if hc > 0:
                    emit_mcopy(t, hc - 1)
                wt("vector", "pe", ev[("gate", t, hc)])
                if hc == 0:
                    wt("vector", "ve", ev[("rbccopy2", t)])
                if hc >= 2:
                    wt("vector", "ae", ev[("silug", t, hc - 2)])
                em("vector", lambda e, hc=hc, s=sem["ve"]: e.tensor_mul(
                    s_sv[hc % 2][:], pb[0][:], s_rbc[:]).then_inc(s, 1))
                ev[("gtmp", t, hc)] = inc("ve")
                wt("vector", "ve", ev[("gtmp", t, hc)])
                em("vector", lambda e, hc=hc, s=sem["ve"]: e.scalar_tensor_tensor(
                    s_sv[hc % 2][:], s_nmr[:], s_wsg[:, hc:hc + 1],
                    s_sv[hc % 2][:], ALU.mult, ALU.add).then_inc(s, 1))
                ev[("gstt", t, hc)] = inc("ve")
                wt("scalar", "ve", ev[("gstt", t, hc)])
                if hc >= 2:
                    wt("scalar", "ve", ev[("msplit", t, hc - 2)])
                em("scalar", lambda e, hc=hc, s=sem["ae"]: e.activation(
                    s_sq[hc % 2][:], s_sv[hc % 2][:], AF.Silu).then_inc(s, 1))
                ev[("silug", t, hc)] = inc("ae")

                wt("vector", "pe", ev[("up", t, hc)])
                em("vector", lambda e, hc=hc, s=sem["ve"]: e.tensor_mul(
                    s_uv[hc % 2][:], pb[1][:], s_rbc[:]).then_inc(s, 1))
                ev[("utmp", t, hc)] = inc("ve")
                wt("vector", "ve", ev[("utmp", t, hc)])
                em("vector", lambda e, hc=hc, s=sem["ve"]: e.scalar_tensor_tensor(
                    s_uv[hc % 2][:], s_nmr[:], s_wsu[:, hc:hc + 1],
                    s_uv[hc % 2][:], ALU.mult, ALU.add).then_inc(s, 1))
                ev[("ustt", t, hc)] = inc("ve")
                if hc > 0:
                    emit_msplit_lo(t, hc - 1)
                wt("vector", "ve", ev[("ustt", t, hc)])
                wt("vector", "ae", ev[("silug", t, hc)])
                em("vector", lambda e, hc=hc, s=sem["ve"]: e.scalar_tensor_tensor(
                    s_sq[hc % 2][:], s_uv[hc % 2][:], INV127,
                    s_sq[hc % 2][:], ALU.mult, ALU.mult).then_inc(s, 1))
                ev[("mstt", t, hc)] = inc("ve")

                if hc == 0 and t > 0:
                    wt("tensor", "ve", ev[("rbccopy2", t)])
                    emit_xt(2)

            emit_mcopy(t, NHC - 1)
            emit_msplit_lo(t, NHC - 1)

            if t == 0:
                wt("tensor", "ve", ev[("wsumuc", 0, NHC - 1)])
                emit_xt(2)

            wt("tensor", "ve", ev[("gtmp", t, NHC - 1)])
            wt("tensor", "ve", ev[("utmp", t, NHC - 1)])
            emit_xt(3)

            if t + 1 < NT:
                emit_hload(t + 1)
            for hc in range(NHC):
                if hc >= 2:
                    wt("sync", "ve", ev[("cwd", t, hc - 2)])
                elif t > 0:
                    wt("sync", "ve", ev[("cwd", t - 1, NHC - 2 + hc)])
                em("sync", lambda e, hc=hc, s=sem["wd%d" % (hc % 2)]: e.dma_start(
                    s_wd8[hc % 2][:], wd_img[hc]).then_inc(s, 16))
                ev[("wddma", t, hc)] = inc("wd%d" % (hc % 2), 16)

                wt("vector", "wd%d" % (hc % 2), ev[("wddma", t, hc)])
                if hc >= 2:
                    wt("vector", "pe", ev[("down", t, hc - 2)])
                elif t > 0:
                    wt("vector", "pe", ev[("down", t - 1, NHC - 2 + hc)])
                em("vector", lambda e, hc=hc, s=sem["ve"]: e.tensor_copy(
                    s_wdf[hc % 2][:], s_wd8[hc % 2][:]).then_inc(s, 1))
                ev[("cwd", t, hc)] = inc("ve")

                if t + 1 < NT:
                    if 2 <= hc < 10:
                        emit_hsplit1(t + 1, hc - 2)
                    elif 10 <= hc < 13:
                        emit_sq(t + 1, 0, 2 * (hc - 10), s_h)
                        emit_sq(t + 1, 0, 2 * (hc - 10) + 1, s_h)

                wt("tensor", "ve", ev[("cwd", t, hc)])
                for q in range(NQ):
                    a, b = f2pair[q]
                    for half in range(2):
                        bank = a if half == 0 else b
                        for p in range(2):
                            msrc = s_mh if p == 0 else s_ml
                            lastofhc = (q == NQ - 1 and half == 1 and p == 1)

                            def mk_down(hc=hc, q=q, half=half, bank=bank, msrc=msrc,
                                        p=p, lastofhc=lastofhc, s=sem["pe"]):
                                def f(e):
                                    i = e.matmul(
                                        pb[bank][:],
                                        msrc[:, 512 * hc + 128 * q:512 * hc + 128 * q + 128],
                                        s_wdf[hc % 2][:, 512 * half:512 * half + 512],
                                        start=False, stop=(hc == NHC - 1 and p == 1))
                                    if lastofhc:
                                        i.then_inc(s, 1)
                                return f
                            em("tensor", mk_down())
                ev[("down", t, hc)] = inc("pe")
            ev[("downall", t)] = cnt["pe"]

            ei = 8 * t
            for q in (3, 0, 1, 2):
                banks = f2pair[q]
                wt("vector", "pe", ev[("downall", t)])
                for half in range(2):
                    par = "st%d" % (ei % 2)
                    if ei >= 2:
                        wt("vector", par, ev[("storeh", ei - 2)])
                    em("vector", lambda e, bank=banks[half], ei=ei, s=sem["ve"]:
                       e.tensor_copy(s_out[ei % 2][:], pb[bank][:]).then_inc(s, 1))
                    ev[("f2evach", ei)] = inc("ve")
                    wt("scalar", "ve", ev[("f2evach", ei)])
                    em("scalar", lambda e, q=q, half=half, t0=t0, ei=ei, s=sem[par]:
                       e.dma_start(
                        out[t0 + 128 * q:t0 + 128 * q + 128,
                            512 * half:512 * half + 512],
                        s_out[ei % 2][:]).then_inc(s, 16))
                    ev[("storeh", ei)] = inc(par, 16)
                    ei += 1
                ev[("f2evac", t, q)] = cnt["ve"]

            if t + 1 < NT:
                emit_ln(t + 1, 0, s_h, prefix_done=True)

        wt("scalar", "st0", cnt["st0"])
        wt("scalar", "st1", cnt["st1"])
        em("gpsimd", lambda e: e.nop())

        with nc.Block() as block:
            @block.tensor
            def _(e):
                for f in ops["tensor"]:
                    f(e)

            @block.vector
            def _(e):
                for f in ops["vector"]:
                    f(e)

            @block.scalar
            def _(e):
                for f in ops["scalar"]:
                    f(e)

            @block.sync
            def _(e):
                for f in ops["sync"]:
                    f(e)

            @block.gpsimd
            def _(e):
                for f in ops["gpsimd"]:
                    f(e)

    return nc


def pack_inputs(inputs):
    h = np.asarray(inputs["L0_attn_hidden_states"])
    w_in = np.asarray(inputs["L0_attn_in_proj_w"])[:C]
    conv_w = np.asarray(inputs["L0_attn_conv_w"])
    conv_b = np.asarray(inputs["L0_attn_conv_b"])
    wo = np.asarray(inputs["L0_attn_out_proj_w"])
    wg = np.asarray(inputs["L0_ffn_w_gate"])
    wu = np.asarray(inputs["L0_ffn_w_up"])
    wd = np.asarray(inputs["L0_ffn_w_down"])

    assert np.little_endian

    def i8view(a):
        return np.ascontiguousarray(a.view(np.int8).reshape(*a.shape, 4)[..., 0])

    w_in_img = np.ascontiguousarray(
        w_in.reshape(NCC, 128, NDC, 128).transpose(0, 3, 2, 1)).reshape(NCC, 128, 1024)
    wo_img = np.ascontiguousarray(
        wo.reshape(NOC, 128, NCC, 128).transpose(0, 3, 2, 1)).reshape(NOC, 128, 2048)
    wg_img = np.ascontiguousarray(
        i8view(wg).reshape(NHC, 128, NDC, 128).transpose(0, 3, 2, 1)).reshape(NHC, 128, 1024)
    wu_img = np.ascontiguousarray(
        i8view(wu).reshape(NHC, 128, NDC, 128).transpose(0, 3, 2, 1)).reshape(NHC, 128, 1024)
    wd_img = np.ascontiguousarray(i8view(wd).T).reshape(NHC, 128, 1024)
    cw_sb = np.ascontiguousarray(
        conv_w[:, 0, :].reshape(NCC, 128, 4).transpose(1, 0, 2)).reshape(128, 64)
    cb_sb = np.ascontiguousarray(conv_b.reshape(NCC, 128).T)
    ident = np.eye(128, dtype=np.float32)

    shared = {
        "w_in_img": w_in_img, "wo_img": wo_img, "wg_img": wg_img,
        "wu_img": wu_img, "wd_img": wd_img, "cw_sb": cw_sb,
        "cb_sb": cb_sb, "ident": ident,
    }
    in_maps = []
    for b in range(h.shape[0]):
        m = dict(shared)
        m["hT"] = np.ascontiguousarray(h[b].T)
        in_maps.append(m)
    return in_maps


def kernel(**inputs):
    in_maps = pack_inputs(inputs)
    T = np.asarray(inputs["L0_attn_hidden_states"]).shape[1]
    nc = build_nc(T)
    res = run_bass_kernel_spmd(nc, in_maps, core_ids=list(range(len(in_maps))))
    return np.stack([res.results[b]["out"] for b in range(len(in_maps))], axis=0)


# version 27
# speedup vs baseline: 1.0614x; 1.0614x over previous
import numpy as np
from contextlib import ExitStack

import concourse.bass as bass
import concourse.mybir as mybir
from concourse.bass_utils import run_bass_kernel_spmd

F32 = mybir.dt.float32
BF = mybir.dt.bfloat16
I8 = mybir.dt.int8
AF = mybir.ActivationFunctionType
ALU = mybir.AluOpType

D, C, H = 1024, 2048, 3072
NDC, NCC, NOC, NHC = 8, 16, 8, 24
TT = 512
NQ = TT // 128
EPS = 1e-5
INVD = 1.0 / D
INV127 = 1.0 / 127.0


def build_nc(T):
    NT = T // TT
    nc = bass.Bass("TRN2", target_bir_lowering=False,
                   dynamic_dma_scratch_size=1024)

    hT = nc.dram_tensor("hT", [D, T], F32, kind="ExternalInput").ap()
    w_in_img = nc.dram_tensor("w_in_img", [NCC, 128, 1024], F32, kind="ExternalInput").ap()
    wo_img = nc.dram_tensor("wo_img", [NOC, 128, 2048], F32, kind="ExternalInput").ap()
    wg_img = nc.dram_tensor("wg_img", [NHC, 128, 1024], I8, kind="ExternalInput").ap()
    wu_img = nc.dram_tensor("wu_img", [NHC, 128, 1024], I8, kind="ExternalInput").ap()
    wd_img = nc.dram_tensor("wd_img", [NHC, 128, 1024], I8, kind="ExternalInput").ap()
    cw_sb = nc.dram_tensor("cw_sb", [128, 64], F32, kind="ExternalInput").ap()
    cb_sb = nc.dram_tensor("cb_sb", [128, 16], F32, kind="ExternalInput").ap()
    ident = nc.dram_tensor("ident", [128, 128], F32, kind="ExternalInput").ap()
    out = nc.dram_tensor("out", [T, D], F32, kind="ExternalOutput").ap()
    wi_hl = nc.dram_tensor("wi_hl", [NCC, 2, 128, 1024], BF, kind="Internal").ap()
    wo_hl = nc.dram_tensor("wo_hl", [NOC * 4, 128, 1024], BF, kind="Internal").ap()

    A = nc.alloc_sbuf_tensor
    s_h = A("s_h", [128, 4096], F32)
    s_lnh = A("s_lnh", [128, 4096], BF)
    s_lnl = A("s_lnl", [128, 4096], BF)
    s_x1 = A("s_x1", [128, 4096], F32)
    s_sh = A("s_sh", [128, 8192], BF)
    s_sl = A("s_sl", [128, 8192], BF)
    s_mh = A("s_mh", [128, 12288], BF)
    s_ml = A("s_ml", [128, 12288], BF)
    s_u = A("s_u", [128, 516], F32)
    s_carry = A("s_carry", [128, 48], F32)
    s_wi = [[A(f"s_wi{i}_{d}", [128, 512], F32) for d in range(2)]
            for i in range(2)]
    s_wiP = [[A(f"s_wiP{i}_{d}", [128, 1024], BF) for d in range(4)]
             for i in range(2)]
    s_wo = [A(f"s_wo{i}", [128, 512], F32) for i in range(2)]
    s_woP = [A(f"s_woP{i}", [128, 1024], BF) for i in range(6)]
    s_wg8 = [A(f"s_wg8{i}", [128, 1024], I8) for i in range(2)]
    s_wu8 = [A(f"s_wu8{i}", [128, 1024], I8) for i in range(2)]
    s_wd8 = [A(f"s_wd8{i}", [128, 1024], I8) for i in range(2)]
    s_wgf = [A(f"s_wgf{i}", [128, 1024], BF) for i in range(2)]
    s_wuf = [A(f"s_wuf{i}", [128, 1024], BF) for i in range(2)]
    s_wdf = [A(f"s_wdf{i}", [128, 1024], BF) for i in range(2)]
    s_out = [A(f"s_out{i}", [128, 512], F32) for i in range(2)]
    s_rbc = A("s_rbc", [128, 512], F32)
    s_nmr = A("s_nmr", [128, 512], F32)
    s_wsum = A("s_wsum", [128, 16], F32)
    s_sq = [A(f"s_sq{i}", [128, 512], F32) for i in range(2)]
    s_sv = [A(f"s_sv{i}", [128, 512], F32) for i in range(2)]
    s_uv = [A(f"s_uv{i}", [128, 512], F32) for i in range(2)]
    s_wsg = A("s_wsg", [128, 24], F32)
    s_wsu = A("s_wsu", [128, 24], F32)
    s_ones_bf = A("s_ones_bf", [128, 1], BF)
    s_id = A("s_id", [128, 128], F32)
    s_cw = A("s_cw", [128, 64], F32)
    s_cb = A("s_cb", [128, 16], F32)
    s_ones = A("s_ones", [128, 1], F32)
    s_ones1 = A("s_ones1", [1, 128], F32)
    s_nones1 = A("s_nones1", [1, 128], F32)
    s_rows = A("s_rows", [1, 2048], F32)
    s_eps = A("s_eps", [1, 1], F32)

    pb = [nc.alloc_psum_tensor(f"pb{i}", [128, 512], F32) for i in range(8)]
    f2pair = [(4, 5), (6, 7), (2, 3), (0, 1)]

    ops = {k: [] for k in ("tensor", "vector", "scalar", "sync", "gpsimd")}

    def em(eng, fn):
        ops[eng].append(fn)

    cnt = {"pe": 0, "ve": 0, "ae": 0, "cst": 0, "h": 0,
           "wiA0": 0, "wiA1": 0, "wiA2": 0, "wiA3": 0,
           "wiB0": 0, "wiB1": 0, "wiB2": 0, "wiB3": 0,
           "wo0": 0, "wo1": 0, "wo2": 0, "wo3": 0, "wo4": 0, "wo5": 0,
           "wg0": 0, "wg1": 0, "wu0": 0, "wu1": 0, "wd0": 0, "wd1": 0,
           "st0": 0, "st1": 0, "gp": 0, "wir0": 0, "wir1": 0,
           "wor0": 0, "wor1": 0}
    ev = {}

    def inc(c, n=1):
        cnt[c] += n
        return cnt[c]

    with ExitStack() as es:
        sem = {k: es.enter_context(nc.semaphore(k)) for k in cnt}

        def wt(eng, c, val):
            if val > 0:
                em(eng, lambda e, s=sem[c], v=val: e.wait_ge(s, v))

        def drain(eng="vector"):
            em(eng, lambda e: e.drain())

        for buf, val in ((s_carry, 0.0), (s_ones, 1.0), (s_ones1, 1.0),
                         (s_nones1, -1.0), (s_eps, EPS), (s_ones_bf, 1.0)):
            em("vector", lambda e, buf=buf, val=val, s=sem["ve"]: e.memset(
                buf[:], val).then_inc(s, 1))
            inc("ve")
        ev["init_ve"] = cnt["ve"]
        for dst, src in ((s_id, ident), (s_cw, cw_sb), (s_cb, cb_sb)):
            em("sync", lambda e, dst=dst, src=src, s=sem["cst"]: e.dma_start(
                dst[:], src).then_inc(s, 16))
            inc("cst", 16)
        ev["consts"] = cnt["cst"]

        for t in range(NT):
            t0 = t * TT

            def emit_hload(tp):
                if tp > 0:
                    wt("sync", "ve", ev[("x1", tp - 1, NOC - 1)])
                for j in range(NDC):
                    em("sync", lambda e, j=j, tp=tp, s=sem["h"]: e.dma_start(
                        s_h[:, 512 * j:512 * j + 512],
                        hT[128 * j:128 * j + 128, tp * TT:tp * TT + TT]
                        ).then_inc(s, 16))
                    inc("h", 16)
                ev[("hldall", tp)] = cnt["h"]

            def emit_widma(tp, cc, half):
                semname = "wi%s%d" % ("AB"[half], cc % 4)
                ipn = "ipA" if half == 0 else "ipB"
                if cc >= 4:
                    wt("sync", "pe", ev[(ipn, tp, cc - 4)])
                elif tp == 1:
                    wt("sync", "pe", ev[(ipn, 0, NCC - 2 + (cc % 2))])
                    if cc < 2:
                        wt("sync", "wir%d" % (cc % 2),
                           ev[("wist", NCC - 2 + cc, 1)])
                else:
                    wt("sync", "pe", ev[(ipn, tp - 1, NCC - 4 + cc)])
                em("sync", lambda e, cc=cc, half=half, s=sem[semname]:
                   e.dma_start(s_wiP[half][cc % 4][:],
                               wi_hl[cc, half]).then_inc(s, 16))
                ev[("widma", tp, cc, half)] = inc(semname, 16)

            def emit_wodma(tp, qi):
                if qi >= 6:
                    wt("sync", "pe", ev[("opQ", tp, qi - 6)])
                elif tp == 1:
                    wt("sync", "pe", ev[("opQ", 0, 4 * NOC - 2 + (qi % 2))])
                    if qi < 2:
                        wt("sync", "wor%d" % (qi % 2),
                           ev[("wost", 4 * NOC - 2 + qi)])
                else:
                    lastq = qi + 6 * ((4 * NOC - 1 - qi) // 6)
                    wt("sync", "pe", ev[("opQ", tp - 1, lastq)])
                em("sync", lambda e, qi=qi, s=sem["wo%d" % (qi % 6)]:
                   e.dma_start(s_woP[qi % 6][:], wo_hl[qi]).then_inc(s, 16))
                ev[("wodma", tp, qi)] = inc("wo%d" % (qi % 6), 16)

            sqb = [[s_sq[0], s_sq[1], s_sv[0], s_sv[1], s_uv[0], s_uv[1]],
                   [s_sq[0], s_sq[1]]]

            def emit_sq(t, which, j, src):
                bufs = sqb[which]
                nb = len(bufs)
                if which == 0:
                    if t > 0 and j < 2:
                        wt("scalar", "ve", ev[("msplit", t - 1, 22 + j)])
                    if t > 0 and j == 2:
                        wt("scalar", "ve", ev[("msplit", t - 1, NHC - 1)])
                    wt("scalar", "h", ev[("hldall", t)])
                else:
                    wt("scalar", "ve", ev[("x1", t, j)])
                if j >= nb:
                    wt("scalar", "pe", ev[("mmsq", t, which, j - nb)])
                em("scalar", lambda e, j=j, src=src, buf=bufs[j % nb],
                   s=sem["ae"]: e.activation(
                    buf[:], src[:, 512 * j:512 * j + 512],
                    AF.Square).then_inc(s, 1))
                ev[("sq", t, which, j)] = inc("ae")

            def emit_pe_statspair(t, which, j, src):
                if which == 0:
                    if t == 0 and j == 0:
                        wt("tensor", "ve", ev["init_ve"])
                    wt("tensor", "h", ev[("hldall", t)])
                    if t > 0 and j == 0:
                        wt("tensor", "ve", ev[("f2evac", t - 1, 3)])
                else:
                    wt("tensor", "ve", ev[("x1", t, j)])
                    if j == 0:
                        wt("tensor", "ve", ev[("var", t, 0)])
                em("tensor", lambda e, j=j, src=src: e.matmul(
                    pb[0][0:1, :], s_ones[:], src[:, 512 * j:512 * j + 512],
                    start=(j == 0), stop=(j == NDC - 1)))
                wt("tensor", "ae", ev[("sq", t, which, j)])
                em("tensor", lambda e, j=j, s=sem["pe"],
                   buf=sqb[which][j % len(sqb[which])]: e.matmul(
                    pb[1][0:1, :], s_ones[:], buf[:],
                    start=(j == 0), stop=(j == NDC - 1)).then_inc(s, 1))
                ev[("mmsq", t, which, j)] = inc("pe")

            def emit_hsplit1(t, j):
                wt("scalar", "h", ev[("hldall", t)])
                wt("scalar", "pe", ev[("up", t - 1, NHC - 1)] if t > 0 else 0)
                em("scalar", lambda e, j=j, s=sem["ae"]: e.copy(
                    s_lnh[:, 512 * j:512 * j + 512],
                    s_h[:, 512 * j:512 * j + 512]).then_inc(s, 1))
                ev[("hhcopy", t, j)] = inc("ae")
                wt("vector", "ae", ev[("hhcopy", t, j)])
                em("vector", lambda e, j=j, s=sem["ve"]: e.tensor_sub(
                    s_lnl[:, 512 * j:512 * j + 512],
                    s_h[:, 512 * j:512 * j + 512],
                    s_lnh[:, 512 * j:512 * j + 512]).then_inc(s, 1))
                ev[("hsplit", t, j)] = inc("ve")

            def emit_ln(t, which, src, prefix_done=False):
                for j in range(NDC):
                    if not (prefix_done and j < 6):
                        emit_sq(t, which, j, src)
                    if which == 0 and not prefix_done:
                        emit_hsplit1(t, j)
                    emit_pe_statspair(t, which, j, src)
                emit_ln_tail(t, which, src)

            def emit_bcast(t, which):
                row_r = s_rows[0:1, 0:512]
                row_mr = s_rows[0:1, 512:1024]
                wt("tensor", "ve", ev[("rows", t, which)])
                if which == 0 and t > 0:
                    wt("tensor", "ve", ev[("f2evac", t - 1, 2)])
                em("tensor", lambda e, row_r=row_r: e.matmul(
                    pb[2][:], s_ones1[:], row_r, start=True, stop=True))
                em("tensor", lambda e, row_mr=row_mr, s=sem["pe"]: e.matmul(
                    pb[3][:], s_nones1[:], row_mr, start=True, stop=True).then_inc(s, 1))
                ev[("bcast", t, which)] = inc("pe")
                wt("vector", "pe", ev[("bcast", t, which)])
                em("vector", lambda e: e.tensor_copy(s_rbc[:], pb[2][:]))
                em("vector", lambda e, s=sem["ve"]: e.tensor_copy(
                    s_nmr[:], pb[3][:]).then_inc(s, 1))
                ev[("rbccopy", t) if which == 0 else ("rbccopy2", t)] = inc("ve")

            def emit_ln_tail(t, which, src):

                row_r = s_rows[0:1, 0:512]
                row_mr = s_rows[0:1, 512:1024]
                row_aux = s_rows[0:1, 1024:1536]
                row_var = s_rows[0:1, 1536:2048]
                wt("vector", "pe", ev[("mmsq", t, which, NDC - 1)])
                em("vector", lambda e, row_aux=row_aux: e.tensor_scalar(
                    row_aux, pb[0][0:1, :], INVD, None, ALU.mult))
                drain()
                em("vector", lambda e, row_aux=row_aux, row_r=row_r: e.tensor_mul(
                    row_r, row_aux, row_aux))
                drain()
                em("vector", lambda e, row_r=row_r, row_var=row_var, s=sem["ve"]:
                   e.scalar_tensor_tensor(
                    row_var, pb[1][0:1, :], INVD, row_r, ALU.mult, ALU.subtract
                    ).then_inc(s, 1))
                ev[("var", t, which)] = inc("ve")
                wt("scalar", "ve", ev[("var", t, which)])
                em("scalar", lambda e, row_r=row_r, row_var=row_var, s=sem["ae"]:
                   e.activation(
                    row_r, row_var, AF.Sqrt, bias=s_eps[0:1, :]).then_inc(s, 1))
                ev[("sd", t, which)] = inc("ae")
                wt("vector", "ae", ev[("sd", t, which)])
                em("vector", lambda e, row_r=row_r, row_var=row_var: e.reciprocal(
                    row_var, row_r))
                drain()
                em("vector", lambda e, row_r=row_r, row_var=row_var,
                   sc=(1.0 if which == 0 else INV127): e.tensor_scalar(
                    row_r, row_var, sc, None, ALU.mult))
                drain()
                em("vector", lambda e, row_r=row_r, row_mr=row_mr, row_aux=row_aux,
                   s=sem["ve"]: e.tensor_mul(row_mr, row_aux, row_r).then_inc(s, 1))
                ev[("rows", t, which)] = inc("ve")

            def emit_ssplit_hi(t, cc):
                wt("scalar", "ae", ev[("silu", t, cc)])
                em("scalar", lambda e, cc=cc, s=sem["ae"]: e.copy(
                    s_sh[:, 512 * cc:512 * cc + 512], s_sv[cc % 2][:]).then_inc(s, 1))
                ev[("shcopy", t, cc)] = inc("ae")

            def emit_ssplit_lo(t, cc):
                wt("gpsimd", "ae", ev[("shcopy", t, cc)])
                em("gpsimd", lambda e, cc=cc, s=sem["gp"]: e.tensor_sub(
                    s_sl[:, 512 * cc:512 * cc + 512], s_sv[cc % 2][:],
                    s_sh[:, 512 * cc:512 * cc + 512]).then_inc(s, 1))
                ev[("ssplit", t, cc)] = inc("gp")

            def emit_silu(t, cc):
                wt("scalar", "ve", ev[("conv", t, cc)])
                if t == 0 and cc == 0:
                    wt("scalar", "cst", ev["consts"])
                if cc >= 2:
                    wt("scalar", "gp", ev[("ssplit", t, cc - 2)])
                em("scalar", lambda e, cc=cc, s=sem["ae"]: e.activation(
                    s_sv[cc % 2][:], s_sq[cc % 2][:],
                    AF.Silu, bias=s_cb[:, cc:cc + 1]).then_inc(s, 1))
                ev[("silu", t, cc)] = inc("ae")
                emit_ssplit_hi(t, cc)

            if t == 0:
                emit_hload(0)
                emit_ln(0, 0, s_h)

            def emit_convblock(t, cc):
                bank = pb[4 + (cc % 2 if t == 0 else cc % 4)]
                wt("vector", "pe", ev[("ip", t, cc)])
                if t == 0 and cc == 0:
                    wt("vector", "cst", ev["consts"])
                wt("vector", "ve", ev[("rbccopy", t)])
                if t == 0:
                    wt("vector", "ve", ev[("wsumcol", t, cc)])
                if cc > 0:
                    wt("vector", "ve", ev[("conv", t, cc - 1)])
                elif t > 0:
                    wt("vector", "ve", ev[("conv", t - 1, NCC - 1)])
                em("vector", lambda e, cc=cc: e.tensor_copy(
                    s_u[:, 0:3], s_carry[:, 3 * cc:3 * cc + 3]))
                em("vector", lambda e, bank=bank: e.tensor_mul(
                    s_u[:, 3:515], bank[:], s_rbc[:]))
                drain()
                em("vector", lambda e, cc=cc, s=sem["ve"]: e.scalar_tensor_tensor(
                    s_u[:, 3:515], s_nmr[:], s_wsum[:, cc:cc + 1], s_u[:, 3:515],
                    ALU.mult, ALU.add).then_inc(s, 1))
                ev[("uevac", t, cc)] = inc("ve")
                drain()
                em("vector", lambda e, cc=cc: e.tensor_copy(
                    s_carry[:, 3 * cc:3 * cc + 3], s_u[:, 512:515]))
                if cc >= 2:
                    wt("vector", "ae", ev[("silu", t, cc - 2)])
                em("vector", lambda e, cc=cc: e.tensor_scalar(
                    s_sq[cc % 2][:], s_u[:, 3:515], s_cw[:, 4 * cc + 3:4 * cc + 4],
                    None, ALU.mult))
                for k in (2, 1, 0):
                    drain()

                    def mk_conv(cc=cc, k=k, s=sem["ve"]):
                        def f(e):
                            i = e.scalar_tensor_tensor(
                                s_sq[cc % 2][:], s_u[:, k:k + 512],
                                s_cw[:, 4 * cc + k:4 * cc + k + 1], s_sq[cc % 2][:],
                                ALU.mult, ALU.add)
                            if k == 0:
                                i.then_inc(s, 1)
                        return f
                    em("vector", mk_conv())
                ev[("conv", t, cc)] = inc("ve")

            for cc in range(NCC):
                if t == 0:
                    for half, semname in ((0, "wiA%d" % (cc % 2)),
                                          (1, "wiB%d" % (cc % 2))):
                        if cc >= 2:
                            wt("sync", "gp", ev[("wsplit", 0, cc - 2, half)])
                            if ("wsumt", 0, cc - 2) in ev:
                                wt("sync", "pe", ev[("wsumt", 0, cc - 2)])
                        em("sync", lambda e, cc=cc, half=half, s=sem[semname]:
                           e.dma_start(
                            s_wi[half][cc % 2][:],
                            w_in_img[cc][:, 512 * half:512 * half + 512]
                            ).then_inc(s, 16))
                        ev[("widma", 0, cc, half)] = inc(semname, 16)

                    for half, semname in ((0, "wiA%d" % (cc % 2)),
                                          (1, "wiB%d" % (cc % 2))):
                        wt("scalar", semname, ev[("widma", 0, cc, half)])
                        if cc >= 2:
                            wt("scalar", "pe",
                               ev[("ipA" if half == 0 else "ipB", 0, cc - 2)])
                            wt("scalar", "gp", ev[("wsplit", 0, cc - 2, half)])
                            wt("scalar", "wir%d" % (cc % 2),
                               ev[("wist", cc - 2, 1)])
                        em("scalar", lambda e, cc=cc, half=half, s=sem["ae"]:
                           e.copy(s_wiP[half][cc % 2][:, 0:512],
                                  s_wi[half][cc % 2][:]).then_inc(s, 1))
                        ev[("whcopy", 0, cc, half)] = inc("ae")
                        wt("gpsimd", "ae", ev[("whcopy", 0, cc, half)])
                        em("gpsimd", lambda e, cc=cc, half=half, s=sem["gp"]:
                           e.tensor_sub(
                            s_wiP[half][cc % 2][:, 512:1024],
                            s_wi[half][cc % 2][:],
                            s_wiP[half][cc % 2][:, 0:512]).then_inc(s, 1))
                        ev[("wsplit", 0, cc, half)] = inc("gp")
                        wt("scalar", "gp", ev[("wsplit", 0, cc, half)])
                        em("scalar", lambda e, cc=cc, half=half,
                           s=sem["wir%d" % (cc % 2)]: e.dma_start(
                            wi_hl[cc, half], s_wiP[half][cc % 2][:]
                            ).then_inc(s, 16))
                        ev[("wist", cc, half)] = inc("wir%d" % (cc % 2), 16)
                else:
                    if cc >= 4:
                        emit_widma(t, cc, 0)
                        emit_widma(t, cc, 1)

                if cc == 1:
                    emit_bcast(t, 0)

                if cc > 0:
                    emit_convblock(t, cc - 1)
                    emit_silu(t, cc - 1)
                    emit_ssplit_lo(t, cc - 1)

                nbk = 2 if t == 0 else 4
                bank = pb[4 + cc % nbk]
                if cc >= nbk:
                    wt("tensor", "ve", ev[("uevac", t, cc - nbk)])
                wdep = cc % 2 if t == 0 else cc % 4
                for j in range(NDC):
                    half, jj = (0, j) if j < 4 else (1, j - 4)
                    if j == 0 or j == 4:
                        if t == 0:
                            wt("tensor", "gp", ev[("wsplit", 0, cc, half)])
                        else:
                            wt("tensor", "wi%s%d" % ("AB"[half], cc % 4),
                               ev[("widma", t, cc, half)])
                    if cc < 2:
                        wt("tensor", "ve", ev[("hsplit", t, j)])
                    for p in range(3):
                        woff = 128 * jj if p < 2 else 512 + 128 * jj
                        xsrc = s_lnl if p == 1 else s_lnh
                        first = (j == 0 and p == 0)
                        last = (j == NDC - 1 and p == 2)
                        mark = (j == 3 and p == 2) or last

                        def mk_ip(j=j, woff=woff, wsrc=s_wiP[half][wdep],
                                  xsrc=xsrc, bank=bank,
                                  first=first, last=last, mark=mark, s=sem["pe"]):
                            def f(e):
                                i = e.matmul(bank[:],
                                             wsrc[:, woff:woff + 128],
                                             xsrc[:, 512 * j:512 * j + 512],
                                             start=first, stop=last)
                                if mark:
                                    i.then_inc(s, 1)
                            return f
                        em("tensor", mk_ip())
                        if j == 3 and p == 2:
                            ev[("ipA", t, cc)] = inc("pe")
                        if last:
                            ev[("ipB", t, cc)] = inc("pe")
                ev[("ip", t, cc)] = cnt["pe"]

                if t == 0:
                    if cc >= 2:
                        wt("tensor", "ve", ev[("wrow", t, cc - 2)])
                    for j8 in range(NDC):
                        half, jj = (0, j8) if j8 < 4 else (1, j8 - 4)

                        def mk_ws(cc=cc, half=half, jj=jj, j8=j8, s=sem["pe"]):
                            def f(e):
                                i = e.matmul(pb[6][0:1, 0:128], s_ones[:],
                                             s_wi[half][cc % 2][:, 128 * jj:128 * jj + 128],
                                             start=(j8 == 0), stop=(j8 == NDC - 1))
                                if j8 == NDC - 1:
                                    i.then_inc(s, 1)
                            return f
                        em("tensor", mk_ws())
                    ev[("wsum", t, cc)] = inc("pe")
                    wt("vector", "pe", ev[("wsum", t, cc)])
                    if cc == 0:
                        wt("vector", "ve", ev[("rows", t, 0)])
                    em("vector", lambda e, s=sem["ve"]: e.tensor_copy(
                        s_rows[0:1, 1024:1152], pb[6][0:1, 0:128]).then_inc(s, 1))
                    ev[("wrow", t, cc)] = inc("ve")
                    wt("tensor", "ve", ev[("wrow", t, cc)])
                    if cc >= 2:
                        wt("tensor", "ve", ev[("wsumcol", t, cc - 2)])
                    em("tensor", lambda e, s=sem["pe"]: e.matmul(
                        pb[7][:, 0:1], s_rows[0:1, 1024:1152], s_ones[0:1, 0:1],
                        start=True, stop=True).then_inc(s, 1))
                    ev[("wsumt", t, cc)] = inc("pe")
                    wt("vector", "pe", ev[("wsumt", t, cc)])
                    em("vector", lambda e, cc=cc, s=sem["ve"]: e.tensor_copy(
                        s_wsum[:, cc:cc + 1], pb[7][:, 0:1]).then_inc(s, 1))
                    ev[("wsumcol", t, cc)] = inc("ve")

            emit_convblock(t, NCC - 1)
            emit_silu(t, NCC - 1)
            emit_ssplit_lo(t, NCC - 1)

            obank = [2, 3, 6, 7]

            def emit_x1add(t, oc):
                wt("vector", "pe", ev[("opB", t, oc)])
                em("vector", lambda e, oc=oc, bank=pb[obank[oc % 4]],
                   s=sem["ve"]: e.tensor_add(
                    s_x1[:, 512 * oc:512 * oc + 512], bank[:],
                    s_h[:, 512 * oc:512 * oc + 512]).then_inc(s, 1))
                ev[("x1", t, oc)] = inc("ve")

            def emit_x1split(t, oc):
                wt("scalar", "ve", ev[("x1", t, oc)])
                em("scalar", lambda e, oc=oc, s=sem["ae"]: e.copy(
                    s_lnh[:, 512 * oc:512 * oc + 512],
                    s_x1[:, 512 * oc:512 * oc + 512]).then_inc(s, 1))
                ev[("x1hi", t, oc)] = inc("ae")
                wt("gpsimd", "ae", ev[("x1hi", t, oc)])
                em("gpsimd", lambda e, oc=oc, s=sem["gp"]: e.tensor_sub(
                    s_lnl[:, 512 * oc:512 * oc + 512],
                    s_x1[:, 512 * oc:512 * oc + 512],
                    s_lnh[:, 512 * oc:512 * oc + 512]).then_inc(s, 1))
                ev[("x1lo", t, oc)] = inc("gp")

            pending_add = None
            for qi in range(4 * NOC):
                oc, qt = divmod(qi, 4)
                if t == 0:
                    if qi >= 2:
                        wt("sync", "ve", ev[("wosplit", 0, qi - 2)])
                    em("sync", lambda e, oc=oc, qt=qt, qi=qi,
                       s=sem["wo%d" % (qi % 2)]:
                       e.dma_start(s_wo[qi % 2][:],
                                   wo_img[oc][:, 512 * qt:512 * qt + 512]
                                   ).then_inc(s, 16))
                    ev[("wodma", 0, qi)] = inc("wo%d" % (qi % 2), 16)

                    wt("scalar", "wo%d" % (qi % 2), ev[("wodma", 0, qi)])
                    if qi >= 2:
                        wt("scalar", "pe", ev[("opQ", 0, qi - 2)])
                        wt("scalar", "ve", ev[("wosplit", 0, qi - 2)])
                        wt("scalar", "wor%d" % (qi % 2), ev[("wost", qi - 2)])
                    em("scalar", lambda e, qi=qi, s=sem["ae"]: e.copy(
                        s_woP[qi % 2][:, 0:512], s_wo[qi % 2][:]).then_inc(s, 1))
                    ev[("wohcopy", 0, qi)] = inc("ae")
                    wt("vector", "ae", ev[("wohcopy", 0, qi)])
                    em("vector", lambda e, qi=qi, s=sem["ve"]: e.tensor_sub(
                        s_woP[qi % 2][:, 512:1024], s_wo[qi % 2][:],
                        s_woP[qi % 2][:, 0:512]).then_inc(s, 1))
                    ev[("wosplit", 0, qi)] = inc("ve")
                    wt("scalar", "ve", ev[("wosplit", 0, qi)])
                    em("scalar", lambda e, qi=qi, s=sem["wor%d" % (qi % 2)]:
                       e.dma_start(wo_hl[qi], s_woP[qi % 2][:]).then_inc(s, 16))
                    ev[("wost", qi)] = inc("wor%d" % (qi % 2), 16)
                else:
                    if qi >= 6:
                        emit_wodma(t, qi)

                if pending_add is not None:
                    emit_x1add(t, pending_add)
                    emit_sq(t, 1, pending_add, s_x1)
                    emit_x1split(t, pending_add)
                    pending_add = None

                bank = pb[obank[oc % 4]]
                if qt == 0:
                    if oc >= 4:
                        wt("tensor", "ve", ev[("x1", t, oc - 4)])
                    elif oc < 2:
                        wt("tensor", "ve", ev[("rbccopy", t)])
                    elif t > 0:
                        wt("tensor", "ve", ev[("uevac", t, NCC - 4 + oc)])
                if t == 0:
                    wt("tensor", "ve", ev[("wosplit", 0, qi)])
                else:
                    wt("tensor", "wo%d" % (qi % 6), ev[("wodma", t, qi)])
                wqdep = qi % 2 if t == 0 else qi % 6
                for kk in range(4):
                    k = 4 * qt + kk
                    if oc == 0:
                        wt("tensor", "gp", ev[("ssplit", t, k)])
                    for p in range(3):
                        woff = 128 * kk + (512 if p == 2 else 0)
                        xsrc = s_sl if p == 1 else s_sh
                        first = (k == 0 and p == 0)
                        last = (k == NCC - 1 and p == 2)
                        mark = (kk == 3 and p == 2)

                        def mk_op(k=k, woff=woff, wsrc=s_woP[wqdep],
                                  xsrc=xsrc, bank=bank,
                                  first=first, last=last, mark=mark, s=sem["pe"]):
                            def f(e):
                                i = e.matmul(bank[:],
                                             wsrc[:, woff:woff + 128],
                                             xsrc[:, 512 * k:512 * k + 512],
                                             start=first, stop=last)
                                if mark:
                                    i.then_inc(s, 1)
                            return f
                        em("tensor", mk_op())
                        if mark:
                            ev[("opQ", t, qi)] = inc("pe")

                if qt == 3:
                    ev[("opB", t, oc)] = cnt["pe"]
                    pending_add = oc
                elif qt == 2 and oc >= 1:
                    emit_pe_statspair(t, 1, oc - 1, s_x1)

            emit_x1add(t, NOC - 1)
            emit_sq(t, 1, NOC - 1, s_x1)
            emit_x1split(t, NOC - 1)
            emit_pe_statspair(t, 1, NOC - 1, s_x1)

            def emit_xt(q):
                a, b = f2pair[q]
                for j in range(NDC):
                    bank, col = (a, j) if j < 4 else (b, j - 4)
                    em("tensor", lambda e, q=q, j=j, bank=bank, col=col: e.matmul(
                        pb[bank][:, 128 * col:128 * col + 128],
                        s_x1[:, 512 * j + 128 * q:512 * j + 128 * q + 128],
                        s_id[:], start=(j % 4 == 0), stop=False))

            if t == 0:
                wt("tensor", "cst", ev["consts"])
            wt("tensor", "ve", ev[("x1", t, NOC - 1)])
            if t > 0:
                for q in range(2):
                    wt("tensor", "ve", ev[("f2evac", t - 1, q)])
            emit_xt(0)
            emit_xt(1)

            def emit_mcopy(t, hc):
                wt("scalar", "ve", ev[("mstt", t, hc)])
                em("scalar", lambda e, hc=hc, s=sem["ae"]: e.copy(
                    s_mh[:, 512 * hc:512 * hc + 512], s_sq[hc % 2][:]).then_inc(s, 1))
                ev[("mhcopy", t, hc)] = inc("ae")

            def emit_msplit_lo(t, hc):
                wt("vector", "ae", ev[("mhcopy", t, hc)])
                em("vector", lambda e, hc=hc, s=sem["ve"]: e.tensor_sub(
                    s_ml[:, 512 * hc:512 * hc + 512], s_sq[hc % 2][:],
                    s_mh[:, 512 * hc:512 * hc + 512]).then_inc(s, 1))
                ev[("msplit", t, hc)] = inc("ve")

            def emit_f1dma_cast(t, hc):
                if hc >= 2:
                    wt("sync", "ve", ev[("cwg", t, hc - 2)])
                elif t > 0:
                    wt("sync", "ve", ev[("cwg", t - 1, NHC - 2 + hc)])
                em("sync", lambda e, hc=hc, s=sem["wg%d" % (hc % 2)]: e.dma_start(
                    s_wg8[hc % 2][:], wg_img[hc]).then_inc(s, 16))
                ev[("wgdma", t, hc)] = inc("wg%d" % (hc % 2), 16)
                if hc >= 2:
                    wt("sync", "ae", ev[("cwu", t, hc - 2)])
                elif t > 0:
                    wt("sync", "ae", ev[("cwu", t - 1, NHC - 2 + hc)])
                em("sync", lambda e, hc=hc, s=sem["wu%d" % (hc % 2)]: e.dma_start(
                    s_wu8[hc % 2][:], wu_img[hc]).then_inc(s, 16))
                ev[("wudma", t, hc)] = inc("wu%d" % (hc % 2), 16)

                def lastw(kind, alt, tp, hcp):
                    return ev[(kind, tp, hcp)] if tp == 0 else ev[(alt, tp, hcp)]

                wt("vector", "wg%d" % (hc % 2), ev[("wgdma", t, hc)])
                if hc >= 2:
                    wt("vector", "pe", lastw("wsumg", "gate", t, hc - 2))
                elif t > 0:
                    wt("vector", "pe", lastw("wsumg", "gate", t - 1, NHC - 2 + hc))
                em("vector", lambda e, hc=hc, s=sem["ve"]: e.tensor_copy(
                    s_wgf[hc % 2][:], s_wg8[hc % 2][:]).then_inc(s, 1))
                ev[("cwg", t, hc)] = inc("ve")

                wt("scalar", "wu%d" % (hc % 2), ev[("wudma", t, hc)])
                if hc >= 2:
                    wt("scalar", "pe", lastw("wsumu", "up", t, hc - 2))
                elif t > 0:
                    wt("scalar", "pe", lastw("wsumu", "up", t - 1, NHC - 2 + hc))
                em("scalar", lambda e, hc=hc, s=sem["ae"]: e.copy(
                    s_wuf[hc % 2][:], s_wu8[hc % 2][:]).then_inc(s, 1))
                ev[("cwu", t, hc)] = inc("ae")

            emit_f1dma_cast(t, 0)
            emit_ln_tail(t, 1, s_x1)
            for hc in range(NHC):
                if hc + 1 < NHC:
                    emit_f1dma_cast(t, hc + 1)
                wt("tensor", "ve", ev[("cwg", t, hc)])
                if hc > 0:
                    wt("tensor", "ve", ev[("gtmp", t, hc - 1)])
                else:
                    wt("tensor", "ve", ev[("var", t, 1)])
                for j in range(NDC):
                    if hc == 0:
                        wt("tensor", "gp", ev[("x1lo", t, j)])
                    for p in range(2):
                        xsrc = s_lnh if p == 0 else s_lnl
                        first = (j == 0 and p == 0)
                        last = (j == NDC - 1 and p == 1)

                        def mk_gate(hc=hc, j=j, xsrc=xsrc, first=first, last=last,
                                    s=sem["pe"]):
                            def f(e):
                                i = e.matmul(pb[0][:],
                                             s_wgf[hc % 2][:, 128 * j:128 * j + 128],
                                             xsrc[:, 512 * j:512 * j + 512],
                                             start=first, stop=last)
                                if last:
                                    i.then_inc(s, 1)
                            return f
                        em("tensor", mk_gate())
                ev[("gate", t, hc)] = inc("pe")
                if hc == 0:
                    emit_bcast(t, 1)
                wt("tensor", "ae", ev[("cwu", t, hc)])
                if hc > 0:
                    wt("tensor", "ve", ev[("utmp", t, hc - 1)])
                for j in range(NDC):
                    for p in range(2):
                        xsrc = s_lnh if p == 0 else s_lnl
                        first = (j == 0 and p == 0)
                        last = (j == NDC - 1 and p == 1)

                        def mk_up(hc=hc, j=j, xsrc=xsrc, first=first, last=last,
                                  s=sem["pe"]):
                            def f(e):
                                i = e.matmul(pb[1][:],
                                             s_wuf[hc % 2][:, 128 * j:128 * j + 128],
                                             xsrc[:, 512 * j:512 * j + 512],
                                             start=first, stop=last)
                                if last:
                                    i.then_inc(s, 1)
                            return f
                        em("tensor", mk_up())
                ev[("up", t, hc)] = inc("pe")

                if t == 0:
                    if hc == 0:
                        wt("tensor", "ve", ev[("rbccopy2", 0)])
                    for wsname, wpar, bank, sdst in (
                            ("wsumg", s_wgf, pb[2], s_wsg),
                            ("wsumu", s_wuf, pb[3], s_wsu)):
                        for j8 in range(NDC):
                            def mk_ws(hc=hc, j8=j8, wpar=wpar, bank=bank,
                                      s=sem["pe"]):
                                def f(e):
                                    i = e.matmul(
                                        bank[:, hc:hc + 1],
                                        wpar[hc % 2][:, 128 * j8:128 * j8 + 128],
                                        s_ones_bf[:],
                                        start=(j8 == 0), stop=(j8 == NDC - 1))
                                    if j8 == NDC - 1:
                                        i.then_inc(s, 1)
                                return f
                            em("tensor", mk_ws())
                        ev[(wsname, 0, hc)] = inc("pe")
                        wt("vector", "pe", ev[(wsname, 0, hc)])
                        em("vector", lambda e, hc=hc, bank=bank, sdst=sdst,
                           s=sem["ve"]: e.tensor_copy(
                            sdst[:, hc:hc + 1], bank[:, hc:hc + 1]).then_inc(s, 1))
                        ev[(wsname + "c", 0, hc)] = inc("ve")

                if hc > 0:
                    emit_mcopy(t, hc - 1)
                wt("vector", "pe", ev[("gate", t, hc)])
                if hc == 0:
                    wt("vector", "ve", ev[("rbccopy2", t)])
                if hc >= 2:
                    wt("vector", "ae", ev[("silug", t, hc - 2)])
                em("vector", lambda e, hc=hc, s=sem["ve"]: e.tensor_mul(
                    s_sv[hc % 2][:], pb[0][:], s_rbc[:]).then_inc(s, 1))
                ev[("gtmp", t, hc)] = inc("ve")
                wt("vector", "ve", ev[("gtmp", t, hc)])
                em("vector", lambda e, hc=hc, s=sem["ve"]: e.scalar_tensor_tensor(
                    s_sv[hc % 2][:], s_nmr[:], s_wsg[:, hc:hc + 1],
                    s_sv[hc % 2][:], ALU.mult, ALU.add).then_inc(s, 1))
                ev[("gstt", t, hc)] = inc("ve")
                wt("scalar", "ve", ev[("gstt", t, hc)])
                if hc >= 2:
                    wt("scalar", "ve", ev[("msplit", t, hc - 2)])
                em("scalar", lambda e, hc=hc, s=sem["ae"]: e.activation(
                    s_sq[hc % 2][:], s_sv[hc % 2][:], AF.Silu).then_inc(s, 1))
                ev[("silug", t, hc)] = inc("ae")

                wt("vector", "pe", ev[("up", t, hc)])
                em("vector", lambda e, hc=hc, s=sem["ve"]: e.tensor_mul(
                    s_uv[hc % 2][:], pb[1][:], s_rbc[:]).then_inc(s, 1))
                ev[("utmp", t, hc)] = inc("ve")
                wt("vector", "ve", ev[("utmp", t, hc)])
                em("vector", lambda e, hc=hc, s=sem["ve"]: e.scalar_tensor_tensor(
                    s_uv[hc % 2][:], s_nmr[:], s_wsu[:, hc:hc + 1],
                    s_uv[hc % 2][:], ALU.mult, ALU.add).then_inc(s, 1))
                ev[("ustt", t, hc)] = inc("ve")
                if hc > 0:
                    emit_msplit_lo(t, hc - 1)
                wt("vector", "ve", ev[("ustt", t, hc)])
                wt("vector", "ae", ev[("silug", t, hc)])
                em("vector", lambda e, hc=hc, s=sem["ve"]: e.scalar_tensor_tensor(
                    s_sq[hc % 2][:], s_uv[hc % 2][:], INV127,
                    s_sq[hc % 2][:], ALU.mult, ALU.mult).then_inc(s, 1))
                ev[("mstt", t, hc)] = inc("ve")

                if hc == 0 and t > 0:
                    wt("tensor", "ve", ev[("rbccopy2", t)])
                    emit_xt(2)

            emit_mcopy(t, NHC - 1)
            emit_msplit_lo(t, NHC - 1)

            if t == 0:
                wt("tensor", "ve", ev[("wsumuc", 0, NHC - 1)])
                emit_xt(2)

            wt("tensor", "ve", ev[("gtmp", t, NHC - 1)])
            wt("tensor", "ve", ev[("utmp", t, NHC - 1)])
            emit_xt(3)

            if t + 1 < NT:
                emit_hload(t + 1)
            for hc in range(NHC):
                if hc >= 2:
                    wt("sync", "ve", ev[("cwd", t, hc - 2)])
                elif t > 0:
                    wt("sync", "ve", ev[("cwd", t - 1, NHC - 2 + hc)])
                em("sync", lambda e, hc=hc, s=sem["wd%d" % (hc % 2)]: e.dma_start(
                    s_wd8[hc % 2][:], wd_img[hc]).then_inc(s, 16))
                ev[("wddma", t, hc)] = inc("wd%d" % (hc % 2), 16)

                wt("vector", "wd%d" % (hc % 2), ev[("wddma", t, hc)])
                if hc >= 2:
                    wt("vector", "pe", ev[("down", t, hc - 2)])
                elif t > 0:
                    wt("vector", "pe", ev[("down", t - 1, NHC - 2 + hc)])
                em("vector", lambda e, hc=hc, s=sem["ve"]: e.tensor_copy(
                    s_wdf[hc % 2][:], s_wd8[hc % 2][:]).then_inc(s, 1))
                ev[("cwd", t, hc)] = inc("ve")

                if t + 1 < NT:
                    if 2 <= hc < 10:
                        emit_hsplit1(t + 1, hc - 2)
                    elif 10 <= hc < 13:
                        emit_sq(t + 1, 0, 2 * (hc - 10), s_h)
                        emit_sq(t + 1, 0, 2 * (hc - 10) + 1, s_h)
                    if hc < 8:
                        emit_widma(t + 1, hc // 2, hc % 2)
                    elif hc < 14:
                        emit_wodma(t + 1, hc - 8)

                wt("tensor", "ve", ev[("cwd", t, hc)])
                for q in range(NQ):
                    a, b = f2pair[q]
                    for half in range(2):
                        bank = a if half == 0 else b
                        for p in range(2):
                            msrc = s_mh if p == 0 else s_ml
                            lastofhc = (q == NQ - 1 and half == 1 and p == 1)

                            def mk_down(hc=hc, q=q, half=half, bank=bank, msrc=msrc,
                                        p=p, lastofhc=lastofhc, s=sem["pe"]):
                                def f(e):
                                    i = e.matmul(
                                        pb[bank][:],
                                        msrc[:, 512 * hc + 128 * q:512 * hc + 128 * q + 128],
                                        s_wdf[hc % 2][:, 512 * half:512 * half + 512],
                                        start=False, stop=(hc == NHC - 1 and p == 1))
                                    if lastofhc:
                                        i.then_inc(s, 1)
                                return f
                            em("tensor", mk_down())
                ev[("down", t, hc)] = inc("pe")
            ev[("downall", t)] = cnt["pe"]

            ei = 8 * t
            for q in (3, 0, 1, 2):
                banks = f2pair[q]
                wt("vector", "pe", ev[("downall", t)])
                for half in range(2):
                    par = "st%d" % (ei % 2)
                    if ei >= 2:
                        wt("vector", par, ev[("storeh", ei - 2)])
                    em("vector", lambda e, bank=banks[half], ei=ei, s=sem["ve"]:
                       e.tensor_copy(s_out[ei % 2][:], pb[bank][:]).then_inc(s, 1))
                    ev[("f2evach", ei)] = inc("ve")
                    wt("scalar", "ve", ev[("f2evach", ei)])
                    em("scalar", lambda e, q=q, half=half, t0=t0, ei=ei, s=sem[par]:
                       e.dma_start(
                        out[t0 + 128 * q:t0 + 128 * q + 128,
                            512 * half:512 * half + 512],
                        s_out[ei % 2][:]).then_inc(s, 16))
                    ev[("storeh", ei)] = inc(par, 16)
                    ei += 1
                ev[("f2evac", t, q)] = cnt["ve"]

            if t + 1 < NT:
                emit_ln(t + 1, 0, s_h, prefix_done=True)

        wt("scalar", "st0", cnt["st0"])
        wt("scalar", "st1", cnt["st1"])
        em("gpsimd", lambda e: e.nop())

        with nc.Block() as block:
            @block.tensor
            def _(e):
                for f in ops["tensor"]:
                    f(e)

            @block.vector
            def _(e):
                for f in ops["vector"]:
                    f(e)

            @block.scalar
            def _(e):
                for f in ops["scalar"]:
                    f(e)

            @block.sync
            def _(e):
                for f in ops["sync"]:
                    f(e)

            @block.gpsimd
            def _(e):
                for f in ops["gpsimd"]:
                    f(e)

    return nc


def pack_inputs(inputs):
    h = np.asarray(inputs["L0_attn_hidden_states"])
    w_in = np.asarray(inputs["L0_attn_in_proj_w"])[:C]
    conv_w = np.asarray(inputs["L0_attn_conv_w"])
    conv_b = np.asarray(inputs["L0_attn_conv_b"])
    wo = np.asarray(inputs["L0_attn_out_proj_w"])
    wg = np.asarray(inputs["L0_ffn_w_gate"])
    wu = np.asarray(inputs["L0_ffn_w_up"])
    wd = np.asarray(inputs["L0_ffn_w_down"])

    assert np.little_endian

    def i8view(a):
        return np.ascontiguousarray(a.view(np.int8).reshape(*a.shape, 4)[..., 0])

    w_in_img = np.ascontiguousarray(
        w_in.reshape(NCC, 128, NDC, 128).transpose(0, 3, 2, 1)).reshape(NCC, 128, 1024)
    wo_img = np.ascontiguousarray(
        wo.reshape(NOC, 128, NCC, 128).transpose(0, 3, 2, 1)).reshape(NOC, 128, 2048)
    wg_img = np.ascontiguousarray(
        i8view(wg).reshape(NHC, 128, NDC, 128).transpose(0, 3, 2, 1)).reshape(NHC, 128, 1024)
    wu_img = np.ascontiguousarray(
        i8view(wu).reshape(NHC, 128, NDC, 128).transpose(0, 3, 2, 1)).reshape(NHC, 128, 1024)
    wd_img = np.ascontiguousarray(i8view(wd).T).reshape(NHC, 128, 1024)
    cw_sb = np.ascontiguousarray(
        conv_w[:, 0, :].reshape(NCC, 128, 4).transpose(1, 0, 2)).reshape(128, 64)
    cb_sb = np.ascontiguousarray(conv_b.reshape(NCC, 128).T)
    ident = np.eye(128, dtype=np.float32)

    shared = {
        "w_in_img": w_in_img, "wo_img": wo_img, "wg_img": wg_img,
        "wu_img": wu_img, "wd_img": wd_img, "cw_sb": cw_sb,
        "cb_sb": cb_sb, "ident": ident,
    }
    in_maps = []
    for b in range(h.shape[0]):
        m = dict(shared)
        m["hT"] = np.ascontiguousarray(h[b].T)
        in_maps.append(m)
    return in_maps


def kernel(**inputs):
    in_maps = pack_inputs(inputs)
    T = np.asarray(inputs["L0_attn_hidden_states"]).shape[1]
    nc = build_nc(T)
    res = run_bass_kernel_spmd(nc, in_maps, core_ids=list(range(len(in_maps))))
    return np.stack([res.results[b]["out"] for b in range(len(in_maps))], axis=0)


# version 28
# speedup vs baseline: 1.0655x; 1.0039x over previous
import numpy as np
from contextlib import ExitStack

import concourse.bass as bass
import concourse.mybir as mybir
from concourse.bass_utils import run_bass_kernel_spmd

F32 = mybir.dt.float32
BF = mybir.dt.bfloat16
I8 = mybir.dt.int8
AF = mybir.ActivationFunctionType
ALU = mybir.AluOpType

D, C, H = 1024, 2048, 3072
NDC, NCC, NOC, NHC = 8, 16, 8, 24
TT = 512
NQ = TT // 128
EPS = 1e-5
INVD = 1.0 / D
INV127 = 1.0 / 127.0


def build_nc(T):
    NT = T // TT
    nc = bass.Bass("TRN2", target_bir_lowering=False,
                   dynamic_dma_scratch_size=1024)

    hT = nc.dram_tensor("hT", [D, T], F32, kind="ExternalInput").ap()
    w_in_img = nc.dram_tensor("w_in_img", [NCC, 128, 1024], F32, kind="ExternalInput").ap()
    wo_img = nc.dram_tensor("wo_img", [NOC, 128, 2048], F32, kind="ExternalInput").ap()
    wg_img = nc.dram_tensor("wg_img", [NHC, 128, 1024], I8, kind="ExternalInput").ap()
    wu_img = nc.dram_tensor("wu_img", [NHC, 128, 1024], I8, kind="ExternalInput").ap()
    wd_img = nc.dram_tensor("wd_img", [NHC, 128, 1024], I8, kind="ExternalInput").ap()
    cw_sb = nc.dram_tensor("cw_sb", [128, 64], F32, kind="ExternalInput").ap()
    cb_sb = nc.dram_tensor("cb_sb", [128, 16], F32, kind="ExternalInput").ap()
    ident = nc.dram_tensor("ident", [128, 128], F32, kind="ExternalInput").ap()
    out = nc.dram_tensor("out", [T, D], F32, kind="ExternalOutput").ap()
    wi_hl = nc.dram_tensor("wi_hl", [NCC, 2, 128, 1024], BF, kind="Internal").ap()
    wo_hl = nc.dram_tensor("wo_hl", [NOC * 4, 128, 1024], BF, kind="Internal").ap()

    A = nc.alloc_sbuf_tensor
    s_h = A("s_h", [128, 4096], F32)
    s_lnh = A("s_lnh", [128, 4096], BF)
    s_lnl = A("s_lnl", [128, 4096], BF)
    s_x1 = A("s_x1", [128, 4096], F32)
    s_sh = A("s_sh", [128, 8192], BF)
    s_sl = A("s_sl", [128, 8192], BF)
    s_mh = A("s_mh", [128, 12288], BF)
    s_ml = A("s_ml", [128, 12288], BF)
    s_u = A("s_u", [128, 516], F32)
    s_carry = A("s_carry", [128, 48], F32)
    s_wi = [[A(f"s_wi{i}_{d}", [128, 512], F32) for d in range(2)]
            for i in range(2)]
    s_wiP = [[A(f"s_wiP{i}_{d}", [128, 1024], BF) for d in range(4)]
             for i in range(2)]
    s_wo = [A(f"s_wo{i}", [128, 512], F32) for i in range(2)]
    s_woP = [A(f"s_woP{i}", [128, 1024], BF) for i in range(6)]
    s_wg8 = [A(f"s_wg8{i}", [128, 1024], I8) for i in range(2)]
    s_wu8 = [A(f"s_wu8{i}", [128, 1024], I8) for i in range(2)]
    s_wd8 = [A(f"s_wd8{i}", [128, 1024], I8) for i in range(2)]
    s_wgf = [A(f"s_wgf{i}", [128, 1024], BF) for i in range(2)]
    s_wuf = [A(f"s_wuf{i}", [128, 1024], BF) for i in range(2)]
    s_wdf = [A(f"s_wdf{i}", [128, 1024], BF) for i in range(2)]
    s_out = [A(f"s_out{i}", [128, 512], F32) for i in range(2)]
    s_rbc = A("s_rbc", [128, 512], F32)
    s_nmr = A("s_nmr", [128, 512], F32)
    s_wsum = A("s_wsum", [128, 16], F32)
    s_sq = [A(f"s_sq{i}", [128, 512], F32) for i in range(2)]
    s_sv = [A(f"s_sv{i}", [128, 512], F32) for i in range(2)]
    s_uv = [A(f"s_uv{i}", [128, 512], F32) for i in range(2)]
    s_wsg = A("s_wsg", [128, 24], F32)
    s_wsu = A("s_wsu", [128, 24], F32)
    s_ones_bf = A("s_ones_bf", [128, 1], BF)
    s_id = A("s_id", [128, 128], F32)
    s_cw = A("s_cw", [128, 64], F32)
    s_cb = A("s_cb", [128, 16], F32)
    s_ones = A("s_ones", [128, 1], F32)
    s_ones1 = A("s_ones1", [1, 128], F32)
    s_nones1 = A("s_nones1", [1, 128], F32)
    s_rows = A("s_rows", [1, 2048], F32)
    s_eps = A("s_eps", [1, 1], F32)

    pb = [nc.alloc_psum_tensor(f"pb{i}", [128, 512], F32) for i in range(8)]
    f2pair = [(4, 5), (6, 7), (2, 3), (0, 1)]

    ops = {k: [] for k in ("tensor", "vector", "scalar", "sync", "gpsimd")}

    def em(eng, fn):
        ops[eng].append(fn)

    cnt = {"pe": 0, "ve": 0, "ae": 0, "cst": 0, "h": 0,
           "wiA0": 0, "wiA1": 0, "wiA2": 0, "wiA3": 0,
           "wiB0": 0, "wiB1": 0, "wiB2": 0, "wiB3": 0,
           "wo0": 0, "wo1": 0, "wo2": 0, "wo3": 0, "wo4": 0, "wo5": 0,
           "wg0": 0, "wg1": 0, "wu0": 0, "wu1": 0, "wd0": 0, "wd1": 0,
           "st0": 0, "st1": 0, "gp": 0, "wir0": 0, "wir1": 0,
           "wor0": 0, "wor1": 0}
    ev = {}

    def inc(c, n=1):
        cnt[c] += n
        return cnt[c]

    with ExitStack() as es:
        sem = {k: es.enter_context(nc.semaphore(k)) for k in cnt}

        def wt(eng, c, val):
            if val > 0:
                em(eng, lambda e, s=sem[c], v=val: e.wait_ge(s, v))

        def drain(eng="vector"):
            em(eng, lambda e: e.drain())

        for buf, val in ((s_carry, 0.0), (s_ones, 1.0), (s_ones1, 1.0),
                         (s_nones1, -1.0), (s_eps, EPS), (s_ones_bf, 1.0)):
            em("vector", lambda e, buf=buf, val=val, s=sem["ve"]: e.memset(
                buf[:], val).then_inc(s, 1))
            inc("ve")
        ev["init_ve"] = cnt["ve"]
        for dst, src in ((s_id, ident), (s_cw, cw_sb), (s_cb, cb_sb)):
            em("sync", lambda e, dst=dst, src=src, s=sem["cst"]: e.dma_start(
                dst[:], src).then_inc(s, 16))
            inc("cst", 16)
        ev["consts"] = cnt["cst"]

        for t in range(NT):
            t0 = t * TT

            def emit_hload(tp):
                if tp > 0:
                    wt("sync", "ve", ev[("x1", tp - 1, NOC - 1)])
                for j in range(NDC):
                    em("sync", lambda e, j=j, tp=tp, s=sem["h"]: e.dma_start(
                        s_h[:, 512 * j:512 * j + 512],
                        hT[128 * j:128 * j + 128, tp * TT:tp * TT + TT]
                        ).then_inc(s, 16))
                    inc("h", 16)
                ev[("hldall", tp)] = cnt["h"]

            def emit_widma(tp, cc, half):
                semname = "wi%s%d" % ("AB"[half], cc % 4)
                ipn = "ipA" if half == 0 else "ipB"
                if cc >= 4:
                    wt("sync", "pe", ev[(ipn, tp, cc - 4)])
                elif tp == 1:
                    wt("sync", "pe", ev[(ipn, 0, NCC - 2 + (cc % 2))])
                    if cc < 2:
                        wt("sync", "wir%d" % (cc % 2),
                           ev[("wist", NCC - 2 + cc, 1)])
                else:
                    wt("sync", "pe", ev[(ipn, tp - 1, NCC - 4 + cc)])
                em("sync", lambda e, cc=cc, half=half, s=sem[semname]:
                   e.dma_start(s_wiP[half][cc % 4][:],
                               wi_hl[cc, half]).then_inc(s, 16))
                ev[("widma", tp, cc, half)] = inc(semname, 16)

            def emit_wodma(tp, qi):
                if qi >= 6:
                    wt("sync", "pe", ev[("opQ", tp, qi - 6)])
                elif tp == 1:
                    wt("sync", "pe", ev[("opQ", 0, 4 * NOC - 2 + (qi % 2))])
                    if qi < 2:
                        wt("sync", "wor%d" % (qi % 2),
                           ev[("wost", 4 * NOC - 2 + qi)])
                else:
                    lastq = qi + 6 * ((4 * NOC - 1 - qi) // 6)
                    wt("sync", "pe", ev[("opQ", tp - 1, lastq)])
                em("sync", lambda e, qi=qi, s=sem["wo%d" % (qi % 6)]:
                   e.dma_start(s_woP[qi % 6][:], wo_hl[qi]).then_inc(s, 16))
                ev[("wodma", tp, qi)] = inc("wo%d" % (qi % 6), 16)

            sqb = [[s_sq[0], s_sq[1], s_sv[0], s_sv[1], s_uv[0], s_uv[1]],
                   [s_sq[0], s_sq[1]]]

            def emit_sq(t, which, j, src):
                bufs = sqb[which]
                nb = len(bufs)
                if which == 0:
                    if t > 0 and j < 2:
                        wt("scalar", "ve", ev[("msplit", t - 1, 22 + j)])
                    if t > 0 and j == 2:
                        wt("scalar", "ve", ev[("msplit", t - 1, NHC - 1)])
                    wt("scalar", "h", ev[("hldall", t)])
                else:
                    wt("scalar", "ve", ev[("x1", t, j)])
                if j >= nb:
                    wt("scalar", "pe", ev[("mmsq", t, which, j - nb)])
                em("scalar", lambda e, j=j, src=src, buf=bufs[j % nb],
                   s=sem["ae"]: e.activation(
                    buf[:], src[:, 512 * j:512 * j + 512],
                    AF.Square).then_inc(s, 1))
                ev[("sq", t, which, j)] = inc("ae")

            def emit_pe_statspair(t, which, j, src):
                if which == 0:
                    if t == 0 and j == 0:
                        wt("tensor", "ve", ev["init_ve"])
                    wt("tensor", "h", ev[("hldall", t)])
                    if t > 0 and j == 0:
                        wt("tensor", "ve", ev[("f2evac", t - 1, 3)])
                else:
                    wt("tensor", "ve", ev[("x1", t, j)])
                    if j == 0:
                        wt("tensor", "ve", ev[("var", t, 0)])
                em("tensor", lambda e, j=j, src=src: e.matmul(
                    pb[0][0:1, :], s_ones[:], src[:, 512 * j:512 * j + 512],
                    start=(j == 0), stop=(j == NDC - 1)))
                wt("tensor", "ae", ev[("sq", t, which, j)])
                em("tensor", lambda e, j=j, s=sem["pe"],
                   buf=sqb[which][j % len(sqb[which])]: e.matmul(
                    pb[1][0:1, :], s_ones[:], buf[:],
                    start=(j == 0), stop=(j == NDC - 1)).then_inc(s, 1))
                ev[("mmsq", t, which, j)] = inc("pe")

            def emit_hsplit1(t, j):
                wt("scalar", "h", ev[("hldall", t)])
                wt("scalar", "pe", ev[("up", t - 1, NHC - 1)] if t > 0 else 0)
                em("scalar", lambda e, j=j, s=sem["ae"]: e.copy(
                    s_lnh[:, 512 * j:512 * j + 512],
                    s_h[:, 512 * j:512 * j + 512]).then_inc(s, 1))
                ev[("hhcopy", t, j)] = inc("ae")
                wt("vector", "ae", ev[("hhcopy", t, j)])
                em("vector", lambda e, j=j, s=sem["ve"]: e.tensor_sub(
                    s_lnl[:, 512 * j:512 * j + 512],
                    s_h[:, 512 * j:512 * j + 512],
                    s_lnh[:, 512 * j:512 * j + 512]).then_inc(s, 1))
                ev[("hsplit", t, j)] = inc("ve")

            def emit_ln(t, which, src, prefix_done=False):
                for j in range(NDC):
                    if not (prefix_done and j < 6):
                        emit_sq(t, which, j, src)
                    if which == 0 and not prefix_done:
                        emit_hsplit1(t, j)
                    emit_pe_statspair(t, which, j, src)
                emit_ln_tail(t, which, src)

            def emit_bcast(t, which):
                row_r = s_rows[0:1, 0:512]
                row_mr = s_rows[0:1, 512:1024]
                wt("tensor", "ve", ev[("rows", t, which)])
                if which == 0 and t > 0:
                    wt("tensor", "ve", ev[("f2evac", t - 1, 2)])
                em("tensor", lambda e, row_r=row_r: e.matmul(
                    pb[2][:], s_ones1[:], row_r, start=True, stop=True))
                em("tensor", lambda e, row_mr=row_mr, s=sem["pe"]: e.matmul(
                    pb[3][:], s_nones1[:], row_mr, start=True, stop=True).then_inc(s, 1))
                ev[("bcast", t, which)] = inc("pe")
                wt("vector", "pe", ev[("bcast", t, which)])
                em("vector", lambda e: e.tensor_copy(s_rbc[:], pb[2][:]))
                em("vector", lambda e, s=sem["ve"]: e.tensor_copy(
                    s_nmr[:], pb[3][:]).then_inc(s, 1))
                ev[("rbccopy", t) if which == 0 else ("rbccopy2", t)] = inc("ve")

            def emit_ln_tail(t, which, src):

                row_r = s_rows[0:1, 0:512]
                row_mr = s_rows[0:1, 512:1024]
                row_aux = s_rows[0:1, 1024:1536]
                row_var = s_rows[0:1, 1536:2048]
                wt("vector", "pe", ev[("mmsq", t, which, NDC - 1)])
                em("vector", lambda e, row_aux=row_aux: e.tensor_scalar(
                    row_aux, pb[0][0:1, :], INVD, None, ALU.mult))
                drain()
                em("vector", lambda e, row_aux=row_aux, row_r=row_r: e.tensor_mul(
                    row_r, row_aux, row_aux))
                drain()
                em("vector", lambda e, row_r=row_r, row_var=row_var, s=sem["ve"]:
                   e.scalar_tensor_tensor(
                    row_var, pb[1][0:1, :], INVD, row_r, ALU.mult, ALU.subtract
                    ).then_inc(s, 1))
                ev[("var", t, which)] = inc("ve")
                wt("scalar", "ve", ev[("var", t, which)])
                em("scalar", lambda e, row_r=row_r, row_var=row_var, s=sem["ae"]:
                   e.activation(
                    row_r, row_var, AF.Sqrt, bias=s_eps[0:1, :]).then_inc(s, 1))
                ev[("sd", t, which)] = inc("ae")
                wt("vector", "ae", ev[("sd", t, which)])
                em("vector", lambda e, row_r=row_r, row_var=row_var: e.reciprocal(
                    row_var, row_r))
                drain()
                em("vector", lambda e, row_r=row_r, row_var=row_var,
                   sc=(1.0 if which == 0 else INV127): e.tensor_scalar(
                    row_r, row_var, sc, None, ALU.mult))
                drain()
                em("vector", lambda e, row_r=row_r, row_mr=row_mr, row_aux=row_aux,
                   s=sem["ve"]: e.tensor_mul(row_mr, row_aux, row_r).then_inc(s, 1))
                ev[("rows", t, which)] = inc("ve")

            def emit_ssplit_hi(t, cc):
                if cc == NCC - 1:
                    return
                wt("scalar", "ae", ev[("silu", t, cc)])
                em("scalar", lambda e, cc=cc, s=sem["ae"]: e.copy(
                    s_sh[:, 512 * cc:512 * cc + 512], s_sv[cc % 2][:]).then_inc(s, 1))
                ev[("shcopy", t, cc)] = inc("ae")

            def emit_ssplit_lo(t, cc):
                if cc == NCC - 1:
                    wt("vector", "ae", ev[("silu", t, cc)])
                    em("vector", lambda e, cc=cc, s=sem["ve"]: e.tensor_copy(
                        s_sh[:, 512 * cc:512 * cc + 512],
                        s_sv[cc % 2][:]).then_inc(s, 1))
                    ev[("shcopy", t, cc)] = inc("ve")
                    wt("vector", "ve", ev[("shcopy", t, cc)])
                    em("vector", lambda e, cc=cc, s=sem["ve"]: e.tensor_sub(
                        s_sl[:, 512 * cc:512 * cc + 512], s_sv[cc % 2][:],
                        s_sh[:, 512 * cc:512 * cc + 512]).then_inc(s, 1))
                    ev[("ssplit", t, cc)] = inc("ve")
                    return
                wt("gpsimd", "ae", ev[("shcopy", t, cc)])
                em("gpsimd", lambda e, cc=cc, s=sem["gp"]: e.tensor_sub(
                    s_sl[:, 512 * cc:512 * cc + 512], s_sv[cc % 2][:],
                    s_sh[:, 512 * cc:512 * cc + 512]).then_inc(s, 1))
                ev[("ssplit", t, cc)] = inc("gp")

            def emit_silu(t, cc):
                wt("scalar", "ve", ev[("conv", t, cc)])
                if t == 0 and cc == 0:
                    wt("scalar", "cst", ev["consts"])
                if cc >= 2:
                    wt("scalar", "gp", ev[("ssplit", t, cc - 2)])
                em("scalar", lambda e, cc=cc, s=sem["ae"]: e.activation(
                    s_sv[cc % 2][:], s_sq[cc % 2][:],
                    AF.Silu, bias=s_cb[:, cc:cc + 1]).then_inc(s, 1))
                ev[("silu", t, cc)] = inc("ae")
                emit_ssplit_hi(t, cc)

            if t == 0:
                emit_hload(0)
                emit_ln(0, 0, s_h)

            def emit_convblock(t, cc):
                bank = pb[4 + (cc % 2 if t == 0 else cc % 4)]
                wt("vector", "pe", ev[("ip", t, cc)])
                if t == 0 and cc == 0:
                    wt("vector", "cst", ev["consts"])
                wt("vector", "ve", ev[("rbccopy", t)])
                if t == 0:
                    wt("vector", "ve", ev[("wsumcol", t, cc)])
                if cc > 0:
                    wt("vector", "ve", ev[("conv", t, cc - 1)])
                elif t > 0:
                    wt("vector", "ve", ev[("conv", t - 1, NCC - 1)])
                em("vector", lambda e, cc=cc: e.tensor_copy(
                    s_u[:, 0:3], s_carry[:, 3 * cc:3 * cc + 3]))
                em("vector", lambda e, bank=bank: e.tensor_mul(
                    s_u[:, 3:515], bank[:], s_rbc[:]))
                drain()
                em("vector", lambda e, cc=cc, s=sem["ve"]: e.scalar_tensor_tensor(
                    s_u[:, 3:515], s_nmr[:], s_wsum[:, cc:cc + 1], s_u[:, 3:515],
                    ALU.mult, ALU.add).then_inc(s, 1))
                ev[("uevac", t, cc)] = inc("ve")
                drain()
                em("vector", lambda e, cc=cc: e.tensor_copy(
                    s_carry[:, 3 * cc:3 * cc + 3], s_u[:, 512:515]))
                if cc >= 2:
                    wt("vector", "ae", ev[("silu", t, cc - 2)])
                em("vector", lambda e, cc=cc: e.tensor_scalar(
                    s_sq[cc % 2][:], s_u[:, 3:515], s_cw[:, 4 * cc + 3:4 * cc + 4],
                    None, ALU.mult))
                for k in (2, 1, 0):
                    drain()

                    def mk_conv(cc=cc, k=k, s=sem["ve"]):
                        def f(e):
                            i = e.scalar_tensor_tensor(
                                s_sq[cc % 2][:], s_u[:, k:k + 512],
                                s_cw[:, 4 * cc + k:4 * cc + k + 1], s_sq[cc % 2][:],
                                ALU.mult, ALU.add)
                            if k == 0:
                                i.then_inc(s, 1)
                        return f
                    em("vector", mk_conv())
                ev[("conv", t, cc)] = inc("ve")

            for cc in range(NCC):
                if t == 0:
                    for half, semname in ((0, "wiA%d" % (cc % 2)),
                                          (1, "wiB%d" % (cc % 2))):
                        if cc >= 2:
                            wt("sync", "gp", ev[("wsplit", 0, cc - 2, half)])
                            if ("wsumt", 0, cc - 2) in ev:
                                wt("sync", "pe", ev[("wsumt", 0, cc - 2)])
                        em("sync", lambda e, cc=cc, half=half, s=sem[semname]:
                           e.dma_start(
                            s_wi[half][cc % 2][:],
                            w_in_img[cc][:, 512 * half:512 * half + 512]
                            ).then_inc(s, 16))
                        ev[("widma", 0, cc, half)] = inc(semname, 16)

                    for half, semname in ((0, "wiA%d" % (cc % 2)),
                                          (1, "wiB%d" % (cc % 2))):
                        wt("scalar", semname, ev[("widma", 0, cc, half)])
                        if cc >= 2:
                            wt("scalar", "pe",
                               ev[("ipA" if half == 0 else "ipB", 0, cc - 2)])
                            wt("scalar", "gp", ev[("wsplit", 0, cc - 2, half)])
                            wt("scalar", "wir%d" % (cc % 2),
                               ev[("wist", cc - 2, 1)])
                        em("scalar", lambda e, cc=cc, half=half, s=sem["ae"]:
                           e.copy(s_wiP[half][cc % 2][:, 0:512],
                                  s_wi[half][cc % 2][:]).then_inc(s, 1))
                        ev[("whcopy", 0, cc, half)] = inc("ae")
                        wt("gpsimd", "ae", ev[("whcopy", 0, cc, half)])
                        em("gpsimd", lambda e, cc=cc, half=half, s=sem["gp"]:
                           e.tensor_sub(
                            s_wiP[half][cc % 2][:, 512:1024],
                            s_wi[half][cc % 2][:],
                            s_wiP[half][cc % 2][:, 0:512]).then_inc(s, 1))
                        ev[("wsplit", 0, cc, half)] = inc("gp")
                        wt("scalar", "gp", ev[("wsplit", 0, cc, half)])
                        em("scalar", lambda e, cc=cc, half=half,
                           s=sem["wir%d" % (cc % 2)]: e.dma_start(
                            wi_hl[cc, half], s_wiP[half][cc % 2][:]
                            ).then_inc(s, 16))
                        ev[("wist", cc, half)] = inc("wir%d" % (cc % 2), 16)
                else:
                    if cc >= 4:
                        emit_widma(t, cc, 0)
                        emit_widma(t, cc, 1)

                if cc == 1:
                    emit_bcast(t, 0)

                if cc > 0:
                    emit_convblock(t, cc - 1)
                    emit_silu(t, cc - 1)
                    emit_ssplit_lo(t, cc - 1)

                nbk = 2 if t == 0 else 4
                bank = pb[4 + cc % nbk]
                if cc >= nbk:
                    wt("tensor", "ve", ev[("uevac", t, cc - nbk)])
                wdep = cc % 2 if t == 0 else cc % 4
                for j in range(NDC):
                    half, jj = (0, j) if j < 4 else (1, j - 4)
                    if j == 0 or j == 4:
                        if t == 0:
                            wt("tensor", "gp", ev[("wsplit", 0, cc, half)])
                        else:
                            wt("tensor", "wi%s%d" % ("AB"[half], cc % 4),
                               ev[("widma", t, cc, half)])
                    if cc < 2:
                        wt("tensor", "ve", ev[("hsplit", t, j)])
                    for p in range(3):
                        woff = 128 * jj if p < 2 else 512 + 128 * jj
                        xsrc = s_lnl if p == 1 else s_lnh
                        first = (j == 0 and p == 0)
                        last = (j == NDC - 1 and p == 2)
                        mark = (j == 3 and p == 2) or last

                        def mk_ip(j=j, woff=woff, wsrc=s_wiP[half][wdep],
                                  xsrc=xsrc, bank=bank,
                                  first=first, last=last, mark=mark, s=sem["pe"]):
                            def f(e):
                                i = e.matmul(bank[:],
                                             wsrc[:, woff:woff + 128],
                                             xsrc[:, 512 * j:512 * j + 512],
                                             start=first, stop=last)
                                if mark:
                                    i.then_inc(s, 1)
                            return f
                        em("tensor", mk_ip())
                        if j == 3 and p == 2:
                            ev[("ipA", t, cc)] = inc("pe")
                        if last:
                            ev[("ipB", t, cc)] = inc("pe")
                ev[("ip", t, cc)] = cnt["pe"]

                if t == 0:
                    if cc >= 2:
                        wt("tensor", "ve", ev[("wrow", t, cc - 2)])
                    for j8 in range(NDC):
                        half, jj = (0, j8) if j8 < 4 else (1, j8 - 4)

                        def mk_ws(cc=cc, half=half, jj=jj, j8=j8, s=sem["pe"]):
                            def f(e):
                                i = e.matmul(pb[6][0:1, 0:128], s_ones[:],
                                             s_wi[half][cc % 2][:, 128 * jj:128 * jj + 128],
                                             start=(j8 == 0), stop=(j8 == NDC - 1))
                                if j8 == NDC - 1:
                                    i.then_inc(s, 1)
                            return f
                        em("tensor", mk_ws())
                    ev[("wsum", t, cc)] = inc("pe")
                    wt("vector", "pe", ev[("wsum", t, cc)])
                    if cc == 0:
                        wt("vector", "ve", ev[("rows", t, 0)])
                    em("vector", lambda e, s=sem["ve"]: e.tensor_copy(
                        s_rows[0:1, 1024:1152], pb[6][0:1, 0:128]).then_inc(s, 1))
                    ev[("wrow", t, cc)] = inc("ve")
                    wt("tensor", "ve", ev[("wrow", t, cc)])
                    if cc >= 2:
                        wt("tensor", "ve", ev[("wsumcol", t, cc - 2)])
                    em("tensor", lambda e, s=sem["pe"]: e.matmul(
                        pb[7][:, 0:1], s_rows[0:1, 1024:1152], s_ones[0:1, 0:1],
                        start=True, stop=True).then_inc(s, 1))
                    ev[("wsumt", t, cc)] = inc("pe")
                    wt("vector", "pe", ev[("wsumt", t, cc)])
                    em("vector", lambda e, cc=cc, s=sem["ve"]: e.tensor_copy(
                        s_wsum[:, cc:cc + 1], pb[7][:, 0:1]).then_inc(s, 1))
                    ev[("wsumcol", t, cc)] = inc("ve")

            emit_convblock(t, NCC - 1)
            emit_silu(t, NCC - 1)
            emit_ssplit_lo(t, NCC - 1)

            obank = [2, 3, 6, 7]

            def emit_x1add(t, oc):
                wt("vector", "pe", ev[("opB", t, oc)])
                em("vector", lambda e, oc=oc, bank=pb[obank[oc % 4]],
                   s=sem["ve"]: e.tensor_add(
                    s_x1[:, 512 * oc:512 * oc + 512], bank[:],
                    s_h[:, 512 * oc:512 * oc + 512]).then_inc(s, 1))
                ev[("x1", t, oc)] = inc("ve")

            def emit_x1split(t, oc):
                wt("scalar", "ve", ev[("x1", t, oc)])
                em("scalar", lambda e, oc=oc, s=sem["ae"]: e.copy(
                    s_lnh[:, 512 * oc:512 * oc + 512],
                    s_x1[:, 512 * oc:512 * oc + 512]).then_inc(s, 1))
                ev[("x1hi", t, oc)] = inc("ae")
                wt("gpsimd", "ae", ev[("x1hi", t, oc)])
                em("gpsimd", lambda e, oc=oc, s=sem["gp"]: e.tensor_sub(
                    s_lnl[:, 512 * oc:512 * oc + 512],
                    s_x1[:, 512 * oc:512 * oc + 512],
                    s_lnh[:, 512 * oc:512 * oc + 512]).then_inc(s, 1))
                ev[("x1lo", t, oc)] = inc("gp")

            pending_add = None
            for qi in range(4 * NOC):
                oc, qt = divmod(qi, 4)
                if t == 0:
                    if qi >= 2:
                        wt("sync", "ve", ev[("wosplit", 0, qi - 2)])
                    em("sync", lambda e, oc=oc, qt=qt, qi=qi,
                       s=sem["wo%d" % (qi % 2)]:
                       e.dma_start(s_wo[qi % 2][:],
                                   wo_img[oc][:, 512 * qt:512 * qt + 512]
                                   ).then_inc(s, 16))
                    ev[("wodma", 0, qi)] = inc("wo%d" % (qi % 2), 16)

                    wt("scalar", "wo%d" % (qi % 2), ev[("wodma", 0, qi)])
                    if qi >= 2:
                        wt("scalar", "pe", ev[("opQ", 0, qi - 2)])
                        wt("scalar", "ve", ev[("wosplit", 0, qi - 2)])
                        wt("scalar", "wor%d" % (qi % 2), ev[("wost", qi - 2)])
                    em("scalar", lambda e, qi=qi, s=sem["ae"]: e.copy(
                        s_woP[qi % 2][:, 0:512], s_wo[qi % 2][:]).then_inc(s, 1))
                    ev[("wohcopy", 0, qi)] = inc("ae")
                    wt("vector", "ae", ev[("wohcopy", 0, qi)])
                    em("vector", lambda e, qi=qi, s=sem["ve"]: e.tensor_sub(
                        s_woP[qi % 2][:, 512:1024], s_wo[qi % 2][:],
                        s_woP[qi % 2][:, 0:512]).then_inc(s, 1))
                    ev[("wosplit", 0, qi)] = inc("ve")
                    wt("scalar", "ve", ev[("wosplit", 0, qi)])
                    em("scalar", lambda e, qi=qi, s=sem["wor%d" % (qi % 2)]:
                       e.dma_start(wo_hl[qi], s_woP[qi % 2][:]).then_inc(s, 16))
                    ev[("wost", qi)] = inc("wor%d" % (qi % 2), 16)
                else:
                    if qi >= 6:
                        emit_wodma(t, qi)

                if pending_add is not None:
                    emit_x1add(t, pending_add)
                    emit_sq(t, 1, pending_add, s_x1)
                    emit_x1split(t, pending_add)
                    pending_add = None

                bank = pb[obank[oc % 4]]
                if qt == 0:
                    if oc >= 4:
                        wt("tensor", "ve", ev[("x1", t, oc - 4)])
                    elif oc < 2:
                        wt("tensor", "ve", ev[("rbccopy", t)])
                    elif t > 0:
                        wt("tensor", "ve", ev[("uevac", t, NCC - 4 + oc)])
                if t == 0:
                    wt("tensor", "ve", ev[("wosplit", 0, qi)])
                else:
                    wt("tensor", "wo%d" % (qi % 6), ev[("wodma", t, qi)])
                wqdep = qi % 2 if t == 0 else qi % 6
                for kk in range(4):
                    k = 4 * qt + kk
                    if oc == 0:
                        wt("tensor", "ve" if k == NCC - 1 else "gp",
                           ev[("ssplit", t, k)])
                    for p in range(3):
                        woff = 128 * kk + (512 if p == 2 else 0)
                        xsrc = s_sl if p == 1 else s_sh
                        first = (k == 0 and p == 0)
                        last = (k == NCC - 1 and p == 2)
                        mark = (kk == 3 and p == 2)

                        def mk_op(k=k, woff=woff, wsrc=s_woP[wqdep],
                                  xsrc=xsrc, bank=bank,
                                  first=first, last=last, mark=mark, s=sem["pe"]):
                            def f(e):
                                i = e.matmul(bank[:],
                                             wsrc[:, woff:woff + 128],
                                             xsrc[:, 512 * k:512 * k + 512],
                                             start=first, stop=last)
                                if mark:
                                    i.then_inc(s, 1)
                            return f
                        em("tensor", mk_op())
                        if mark:
                            ev[("opQ", t, qi)] = inc("pe")

                if qt == 3:
                    ev[("opB", t, oc)] = cnt["pe"]
                    pending_add = oc
                elif qt == 2 and oc >= 1:
                    emit_pe_statspair(t, 1, oc - 1, s_x1)

            emit_x1add(t, NOC - 1)
            emit_sq(t, 1, NOC - 1, s_x1)
            emit_x1split(t, NOC - 1)
            emit_pe_statspair(t, 1, NOC - 1, s_x1)

            def emit_xt(q):
                a, b = f2pair[q]
                for j in range(NDC):
                    bank, col = (a, j) if j < 4 else (b, j - 4)
                    em("tensor", lambda e, q=q, j=j, bank=bank, col=col: e.matmul(
                        pb[bank][:, 128 * col:128 * col + 128],
                        s_x1[:, 512 * j + 128 * q:512 * j + 128 * q + 128],
                        s_id[:], start=(j % 4 == 0), stop=False))

            if t == 0:
                wt("tensor", "cst", ev["consts"])
            wt("tensor", "ve", ev[("x1", t, NOC - 1)])
            if t > 0:
                for q in range(2):
                    wt("tensor", "ve", ev[("f2evac", t - 1, q)])
            emit_xt(0)
            emit_xt(1)

            def emit_mcopy(t, hc):
                wt("scalar", "ve", ev[("mstt", t, hc)])
                em("scalar", lambda e, hc=hc, s=sem["ae"]: e.copy(
                    s_mh[:, 512 * hc:512 * hc + 512], s_sq[hc % 2][:]).then_inc(s, 1))
                ev[("mhcopy", t, hc)] = inc("ae")

            def emit_msplit_lo(t, hc):
                wt("vector", "ae", ev[("mhcopy", t, hc)])
                em("vector", lambda e, hc=hc, s=sem["ve"]: e.tensor_sub(
                    s_ml[:, 512 * hc:512 * hc + 512], s_sq[hc % 2][:],
                    s_mh[:, 512 * hc:512 * hc + 512]).then_inc(s, 1))
                ev[("msplit", t, hc)] = inc("ve")

            def emit_f1dma_cast(t, hc):
                if hc >= 2:
                    wt("sync", "ve", ev[("cwg", t, hc - 2)])
                elif t > 0:
                    wt("sync", "ve", ev[("cwg", t - 1, NHC - 2 + hc)])
                em("sync", lambda e, hc=hc, s=sem["wg%d" % (hc % 2)]: e.dma_start(
                    s_wg8[hc % 2][:], wg_img[hc]).then_inc(s, 16))
                ev[("wgdma", t, hc)] = inc("wg%d" % (hc % 2), 16)
                if hc >= 2:
                    wt("sync", "ae", ev[("cwu", t, hc - 2)])
                elif t > 0:
                    wt("sync", "ae", ev[("cwu", t - 1, NHC - 2 + hc)])
                em("sync", lambda e, hc=hc, s=sem["wu%d" % (hc % 2)]: e.dma_start(
                    s_wu8[hc % 2][:], wu_img[hc]).then_inc(s, 16))
                ev[("wudma", t, hc)] = inc("wu%d" % (hc % 2), 16)

                def lastw(kind, alt, tp, hcp):
                    return ev[(kind, tp, hcp)] if tp == 0 else ev[(alt, tp, hcp)]

                wt("vector", "wg%d" % (hc % 2), ev[("wgdma", t, hc)])
                if hc >= 2:
                    wt("vector", "pe", lastw("wsumg", "gate", t, hc - 2))
                elif t > 0:
                    wt("vector", "pe", lastw("wsumg", "gate", t - 1, NHC - 2 + hc))
                em("vector", lambda e, hc=hc, s=sem["ve"]: e.tensor_copy(
                    s_wgf[hc % 2][:], s_wg8[hc % 2][:]).then_inc(s, 1))
                ev[("cwg", t, hc)] = inc("ve")

                wt("scalar", "wu%d" % (hc % 2), ev[("wudma", t, hc)])
                if hc >= 2:
                    wt("scalar", "pe", lastw("wsumu", "up", t, hc - 2))
                elif t > 0:
                    wt("scalar", "pe", lastw("wsumu", "up", t - 1, NHC - 2 + hc))
                em("scalar", lambda e, hc=hc, s=sem["ae"]: e.copy(
                    s_wuf[hc % 2][:], s_wu8[hc % 2][:]).then_inc(s, 1))
                ev[("cwu", t, hc)] = inc("ae")

            emit_f1dma_cast(t, 0)
            emit_ln_tail(t, 1, s_x1)
            for hc in range(NHC):
                if hc + 1 < NHC:
                    emit_f1dma_cast(t, hc + 1)
                wt("tensor", "ve", ev[("cwg", t, hc)])
                if hc > 0:
                    wt("tensor", "ve", ev[("gtmp", t, hc - 1)])
                else:
                    wt("tensor", "ve", ev[("var", t, 1)])
                for j in range(NDC):
                    if hc == 0:
                        wt("tensor", "gp", ev[("x1lo", t, j)])
                    for p in range(2):
                        xsrc = s_lnh if p == 0 else s_lnl
                        first = (j == 0 and p == 0)
                        last = (j == NDC - 1 and p == 1)

                        def mk_gate(hc=hc, j=j, xsrc=xsrc, first=first, last=last,
                                    s=sem["pe"]):
                            def f(e):
                                i = e.matmul(pb[0][:],
                                             s_wgf[hc % 2][:, 128 * j:128 * j + 128],
                                             xsrc[:, 512 * j:512 * j + 512],
                                             start=first, stop=last)
                                if last:
                                    i.then_inc(s, 1)
                            return f
                        em("tensor", mk_gate())
                ev[("gate", t, hc)] = inc("pe")
                if hc == 0:
                    emit_bcast(t, 1)
                wt("tensor", "ae", ev[("cwu", t, hc)])
                if hc > 0:
                    wt("tensor", "ve", ev[("utmp", t, hc - 1)])
                for j in range(NDC):
                    for p in range(2):
                        xsrc = s_lnh if p == 0 else s_lnl
                        first = (j == 0 and p == 0)
                        last = (j == NDC - 1 and p == 1)

                        def mk_up(hc=hc, j=j, xsrc=xsrc, first=first, last=last,
                                  s=sem["pe"]):
                            def f(e):
                                i = e.matmul(pb[1][:],
                                             s_wuf[hc % 2][:, 128 * j:128 * j + 128],
                                             xsrc[:, 512 * j:512 * j + 512],
                                             start=first, stop=last)
                                if last:
                                    i.then_inc(s, 1)
                            return f
                        em("tensor", mk_up())
                ev[("up", t, hc)] = inc("pe")

                if t == 0:
                    if hc == 0:
                        wt("tensor", "ve", ev[("rbccopy2", 0)])
                    for wsname, wpar, bank, sdst in (
                            ("wsumg", s_wgf, pb[2], s_wsg),
                            ("wsumu", s_wuf, pb[3], s_wsu)):
                        for j8 in range(NDC):
                            def mk_ws(hc=hc, j8=j8, wpar=wpar, bank=bank,
                                      s=sem["pe"]):
                                def f(e):
                                    i = e.matmul(
                                        bank[:, hc:hc + 1],
                                        wpar[hc % 2][:, 128 * j8:128 * j8 + 128],
                                        s_ones_bf[:],
                                        start=(j8 == 0), stop=(j8 == NDC - 1))
                                    if j8 == NDC - 1:
                                        i.then_inc(s, 1)
                                return f
                            em("tensor", mk_ws())
                        ev[(wsname, 0, hc)] = inc("pe")
                        wt("vector", "pe", ev[(wsname, 0, hc)])
                        em("vector", lambda e, hc=hc, bank=bank, sdst=sdst,
                           s=sem["ve"]: e.tensor_copy(
                            sdst[:, hc:hc + 1], bank[:, hc:hc + 1]).then_inc(s, 1))
                        ev[(wsname + "c", 0, hc)] = inc("ve")

                if hc > 0:
                    emit_mcopy(t, hc - 1)
                wt("vector", "pe", ev[("gate", t, hc)])
                if hc == 0:
                    wt("vector", "ve", ev[("rbccopy2", t)])
                if hc >= 2:
                    wt("vector", "ae", ev[("silug", t, hc - 2)])
                em("vector", lambda e, hc=hc, s=sem["ve"]: e.tensor_mul(
                    s_sv[hc % 2][:], pb[0][:], s_rbc[:]).then_inc(s, 1))
                ev[("gtmp", t, hc)] = inc("ve")
                wt("vector", "ve", ev[("gtmp", t, hc)])
                em("vector", lambda e, hc=hc, s=sem["ve"]: e.scalar_tensor_tensor(
                    s_sv[hc % 2][:], s_nmr[:], s_wsg[:, hc:hc + 1],
                    s_sv[hc % 2][:], ALU.mult, ALU.add).then_inc(s, 1))
                ev[("gstt", t, hc)] = inc("ve")
                wt("scalar", "ve", ev[("gstt", t, hc)])
                if hc >= 2:
                    wt("scalar", "ve", ev[("msplit", t, hc - 2)])
                em("scalar", lambda e, hc=hc, s=sem["ae"]: e.activation(
                    s_sq[hc % 2][:], s_sv[hc % 2][:], AF.Silu).then_inc(s, 1))
                ev[("silug", t, hc)] = inc("ae")

                wt("vector", "pe", ev[("up", t, hc)])
                em("vector", lambda e, hc=hc, s=sem["ve"]: e.tensor_mul(
                    s_uv[hc % 2][:], pb[1][:], s_rbc[:]).then_inc(s, 1))
                ev[("utmp", t, hc)] = inc("ve")
                wt("vector", "ve", ev[("utmp", t, hc)])
                em("vector", lambda e, hc=hc, s=sem["ve"]: e.scalar_tensor_tensor(
                    s_uv[hc % 2][:], s_nmr[:], s_wsu[:, hc:hc + 1],
                    s_uv[hc % 2][:], ALU.mult, ALU.add).then_inc(s, 1))
                ev[("ustt", t, hc)] = inc("ve")
                if hc > 0:
                    emit_msplit_lo(t, hc - 1)
                wt("vector", "ve", ev[("ustt", t, hc)])
                wt("vector", "ae", ev[("silug", t, hc)])
                em("vector", lambda e, hc=hc, s=sem["ve"]: e.scalar_tensor_tensor(
                    s_sq[hc % 2][:], s_uv[hc % 2][:], INV127,
                    s_sq[hc % 2][:], ALU.mult, ALU.mult).then_inc(s, 1))
                ev[("mstt", t, hc)] = inc("ve")

                if hc == 0 and t > 0:
                    wt("tensor", "ve", ev[("rbccopy2", t)])
                    emit_xt(2)

            emit_mcopy(t, NHC - 1)
            emit_msplit_lo(t, NHC - 1)

            if t == 0:
                wt("tensor", "ve", ev[("wsumuc", 0, NHC - 1)])
                emit_xt(2)

            wt("tensor", "ve", ev[("gtmp", t, NHC - 1)])
            wt("tensor", "ve", ev[("utmp", t, NHC - 1)])
            emit_xt(3)

            if t + 1 < NT:
                emit_hload(t + 1)
            for hc in range(NHC):
                if hc >= 2:
                    wt("sync", "ve", ev[("cwd", t, hc - 2)])
                elif t > 0:
                    wt("sync", "ve", ev[("cwd", t - 1, NHC - 2 + hc)])
                em("sync", lambda e, hc=hc, s=sem["wd%d" % (hc % 2)]: e.dma_start(
                    s_wd8[hc % 2][:], wd_img[hc]).then_inc(s, 16))
                ev[("wddma", t, hc)] = inc("wd%d" % (hc % 2), 16)

                wt("vector", "wd%d" % (hc % 2), ev[("wddma", t, hc)])
                if hc >= 2:
                    wt("vector", "pe", ev[("down", t, hc - 2)])
                elif t > 0:
                    wt("vector", "pe", ev[("down", t - 1, NHC - 2 + hc)])
                em("vector", lambda e, hc=hc, s=sem["ve"]: e.tensor_copy(
                    s_wdf[hc % 2][:], s_wd8[hc % 2][:]).then_inc(s, 1))
                ev[("cwd", t, hc)] = inc("ve")

                if t + 1 < NT:
                    if 2 <= hc < 10:
                        emit_hsplit1(t + 1, hc - 2)
                    elif 10 <= hc < 13:
                        emit_sq(t + 1, 0, 2 * (hc - 10), s_h)
                        emit_sq(t + 1, 0, 2 * (hc - 10) + 1, s_h)
                    if hc < 8:
                        emit_widma(t + 1, hc // 2, hc % 2)
                    elif hc < 14:
                        emit_wodma(t + 1, hc - 8)

                wt("tensor", "ve", ev[("cwd", t, hc)])
                for q in range(NQ):
                    a, b = f2pair[q]
                    for half in range(2):
                        bank = a if half == 0 else b
                        for p in range(2):
                            msrc = s_mh if p == 0 else s_ml
                            lastofhc = (q == NQ - 1 and half == 1 and p == 1)

                            def mk_down(hc=hc, q=q, half=half, bank=bank, msrc=msrc,
                                        p=p, lastofhc=lastofhc, s=sem["pe"]):
                                def f(e):
                                    i = e.matmul(
                                        pb[bank][:],
                                        msrc[:, 512 * hc + 128 * q:512 * hc + 128 * q + 128],
                                        s_wdf[hc % 2][:, 512 * half:512 * half + 512],
                                        start=False, stop=(hc == NHC - 1 and p == 1))
                                    if lastofhc:
                                        i.then_inc(s, 1)
                                return f
                            em("tensor", mk_down())
                ev[("down", t, hc)] = inc("pe")
            ev[("downall", t)] = cnt["pe"]

            ei = 8 * t
            for q in (3, 0, 1, 2):
                banks = f2pair[q]
                wt("vector", "pe", ev[("downall", t)])
                for half in range(2):
                    par = "st%d" % (ei % 2)
                    if ei >= 2:
                        wt("vector", par, ev[("storeh", ei - 2)])
                    em("vector", lambda e, bank=banks[half], ei=ei, s=sem["ve"]:
                       e.tensor_copy(s_out[ei % 2][:], pb[bank][:]).then_inc(s, 1))
                    ev[("f2evach", ei)] = inc("ve")
                    wt("scalar", "ve", ev[("f2evach", ei)])
                    em("scalar", lambda e, q=q, half=half, t0=t0, ei=ei, s=sem[par]:
                       e.dma_start(
                        out[t0 + 128 * q:t0 + 128 * q + 128,
                            512 * half:512 * half + 512],
                        s_out[ei % 2][:]).then_inc(s, 16))
                    ev[("storeh", ei)] = inc(par, 16)
                    ei += 1
                ev[("f2evac", t, q)] = cnt["ve"]

            if t + 1 < NT:
                emit_ln(t + 1, 0, s_h, prefix_done=True)

        wt("scalar", "st0", cnt["st0"])
        wt("scalar", "st1", cnt["st1"])
        em("gpsimd", lambda e: e.nop())

        with nc.Block() as block:
            @block.tensor
            def _(e):
                for f in ops["tensor"]:
                    f(e)

            @block.vector
            def _(e):
                for f in ops["vector"]:
                    f(e)

            @block.scalar
            def _(e):
                for f in ops["scalar"]:
                    f(e)

            @block.sync
            def _(e):
                for f in ops["sync"]:
                    f(e)

            @block.gpsimd
            def _(e):
                for f in ops["gpsimd"]:
                    f(e)

    return nc


def pack_inputs(inputs):
    h = np.asarray(inputs["L0_attn_hidden_states"])
    w_in = np.asarray(inputs["L0_attn_in_proj_w"])[:C]
    conv_w = np.asarray(inputs["L0_attn_conv_w"])
    conv_b = np.asarray(inputs["L0_attn_conv_b"])
    wo = np.asarray(inputs["L0_attn_out_proj_w"])
    wg = np.asarray(inputs["L0_ffn_w_gate"])
    wu = np.asarray(inputs["L0_ffn_w_up"])
    wd = np.asarray(inputs["L0_ffn_w_down"])

    assert np.little_endian

    def i8view(a):
        return np.ascontiguousarray(a.view(np.int8).reshape(*a.shape, 4)[..., 0])

    w_in_img = np.ascontiguousarray(
        w_in.reshape(NCC, 128, NDC, 128).transpose(0, 3, 2, 1)).reshape(NCC, 128, 1024)
    wo_img = np.ascontiguousarray(
        wo.reshape(NOC, 128, NCC, 128).transpose(0, 3, 2, 1)).reshape(NOC, 128, 2048)
    wg_img = np.ascontiguousarray(
        i8view(wg).reshape(NHC, 128, NDC, 128).transpose(0, 3, 2, 1)).reshape(NHC, 128, 1024)
    wu_img = np.ascontiguousarray(
        i8view(wu).reshape(NHC, 128, NDC, 128).transpose(0, 3, 2, 1)).reshape(NHC, 128, 1024)
    wd_img = np.ascontiguousarray(i8view(wd).T).reshape(NHC, 128, 1024)
    cw_sb = np.ascontiguousarray(
        conv_w[:, 0, :].reshape(NCC, 128, 4).transpose(1, 0, 2)).reshape(128, 64)
    cb_sb = np.ascontiguousarray(conv_b.reshape(NCC, 128).T)
    ident = np.eye(128, dtype=np.float32)

    shared = {
        "w_in_img": w_in_img, "wo_img": wo_img, "wg_img": wg_img,
        "wu_img": wu_img, "wd_img": wd_img, "cw_sb": cw_sb,
        "cb_sb": cb_sb, "ident": ident,
    }
    in_maps = []
    for b in range(h.shape[0]):
        m = dict(shared)
        m["hT"] = np.ascontiguousarray(h[b].T)
        in_maps.append(m)
    return in_maps


def kernel(**inputs):
    in_maps = pack_inputs(inputs)
    T = np.asarray(inputs["L0_attn_hidden_states"]).shape[1]
    nc = build_nc(T)
    res = run_bass_kernel_spmd(nc, in_maps, core_ids=list(range(len(in_maps))))
    return np.stack([res.results[b]["out"] for b in range(len(in_maps))], axis=0)
